# version 01 (final)
# baseline (speedup 1.0000x reference)
import contextlib
import numpy as np
import ml_dtypes
import concourse.bass as bass
import concourse.mybir as mybir
from concourse.bass_utils import run_bass_kernel_spmd

F32 = mybir.dt.float32
BF16 = mybir.dt.bfloat16
AF = mybir.ActivationFunctionType
ALU = mybir.AluOpType
AX = mybir.AxisListType
BF = ml_dtypes.bfloat16

NCORES = 8
B, S, D = 2, 8192, 1024
TC = S // 4
DT = D // 128
LN_EPS = 1e-5


class Sem:
    def __init__(self, handle):
        self.h = handle
        self.n = 0

    def inc(self, instr, k=1):
        instr.then_inc(self.h, k)
        self.n += k
        return self.n


class Ctx:
    def __init__(self, psum_banks=6):
        self.nc = bass.Bass("TRN2", target_bir_lowering=False)
        self.st = contextlib.ExitStack()
        self.sems = {}
        for nm in ["ld", "wl", "st", "mm", "sc", "vec", "v2"]:
            self.sems[nm] = Sem(self.st.enter_context(self.nc.semaphore(nm)))
        self.ps = [self.st.enter_context(
            self.nc.psum_tensor(f"ps{i}", [128, 512], F32)).ap()
            for i in range(psum_banks)]
        self.ones_col = self.sb("c_ones_col", [128, 1], F32)
        self.ones_row = self.sb("c_ones_row", [1, 128], F32)
        self.ones_row_bf = self.sb("c_ones_row_bf", [1, 128], BF16)
        self.flushbuf = self.sb("c_flush", [128, 256], F32)
        self.brow_hi = self.sb("c_brow_hi", [1, 512], BF16)
        self.brow_lo = self.sb("c_brow_lo", [1, 512], BF16)
        self.brow_32 = self.sb("c_brow_32", [1, 512], F32)

    def sb(self, name, shape, dt):
        return self.nc.alloc_sbuf_tensor(name, shape, dt).ap()

    def fl(self, v):
        return v.tensor_scalar_mul(self.flushbuf[:, :], self.flushbuf[:, :], 1.0)

    def tiles(self, name, n, w, dt):
        base = self.sb(name, [128, n * w], dt)
        return base, [base[:, i * w:(i + 1) * w] for i in range(n)]

    def close(self):
        self.st.close()


def dma_tiles(q_engine, sem, base_ap, n, src_2d):
    q_engine.dma_start(
        out=base_ap.rearrange("p (a t) -> p a t", a=n),
        in_=src_2d.rearrange("(a p) t -> p a t", p=128),
    ).then_inc(sem.h, 16)
    sem.n += 16




def emit_bcast2_prep(ctx, v, src_row, w):
    v.tensor_copy(ctx.brow_hi[0:1, 0:w], src_row)
    v.tensor_copy(ctx.brow_32[0:1, 0:w], ctx.brow_hi[0:1, 0:w])
    v.tensor_sub(ctx.brow_32[0:1, 0:w], src_row, ctx.brow_32[0:1, 0:w])
    return v.tensor_copy(ctx.brow_lo[0:1, 0:w], ctx.brow_32[0:1, 0:w])


def emit_bcast2_mm(ctx, t, ps_target, w, nrows=128):
    t.matmul(ps_target[0:nrows, 0:w], ctx.ones_row_bf[0:1, 0:nrows],
             ctx.brow_hi[0:1, 0:w], start=True, stop=False)
    return t.matmul(ps_target[0:nrows, 0:w], ctx.ones_row_bf[0:1, 0:nrows],
                    ctx.brow_lo[0:1, 0:w], start=False, stop=True)



def emit_ln_chunk(ctx, bk, X, XN, xoff, noff, w, psA, psB, scr, srow):
    sm = ctx.sems

    @bk.tensor
    def _(t):
        t.wait_ge(sm["vec"].h, sm["vec"].n)
        for kt in range(DT):
            sm["mm"].inc(t.matmul(psA[0:1, 0:w], ctx.ones_col[:, :], X[kt][:, xoff:xoff + w],
                     start=(kt == 0), stop=(kt == DT - 1)))

    for kt in range(DT):
        @bk.scalar
        def _(sc, kt=kt):
            sc.wait_ge(sm["vec"].h, sm["vec"].n)
            sc.wait_ge(sm["mm"].h, sm["mm"].n)
            sm["sc"].inc(sc.activation(scr[:, 0:w], X[kt][:, xoff:xoff + w], AF.Square))

        @bk.tensor
        def _(t, kt=kt):
            t.wait_ge(sm["sc"].h, sm["sc"].n)
            sm["mm"].inc(t.matmul(psA[32:33, 0:w], ctx.ones_col[:, :], scr[:, 0:w],
                                  start=(kt == 0), stop=(kt == DT - 1)))

    @bk.vector
    def _(v):
        v.wait_ge(sm["mm"].h, sm["mm"].n)
        v.tensor_scalar_mul(srow[0:1, 0:w], psA[0:1, 0:w], 1.0 / D)
        v.tensor_scalar_mul(srow[0:1, 512:512 + w], psA[32:33, 0:w], 1.0 / D)
        v.tensor_mul(srow[0:1, 1024:1024 + w], srow[0:1, 0:w], srow[0:1, 0:w])
        sm["vec"].inc(v.tensor_sub(srow[0:1, 512:512 + w], srow[0:1, 512:512 + w],
                     srow[0:1, 1024:1024 + w]))

    @bk.vector
    def _(v):
        sm["vec"].inc(v.tensor_scalar_add(srow[0:1, 512:512 + w], srow[0:1, 512:512 + w], LN_EPS))

    @bk.scalar
    def _(sc):
        sc.wait_ge(sm["vec"].h, sm["vec"].n)
        sc.activation(srow[0:1, 512:512 + w], srow[0:1, 512:512 + w], AF.Ln)
        sm["sc"].inc(sc.activation(srow[0:1, 512:512 + w], srow[0:1, 512:512 + w],
                      AF.Exp, scale=-0.5))

    @bk.tensor
    def _(t):
        t.wait_ge(sm["sc"].h, sm["sc"].n)
        t.matmul(psB[:, 0:w], ctx.ones_row[:, :], srow[0:1, 0:w],
                 start=True, stop=True)
        sm["mm"].inc(t.matmul(psA[:, 0:w], ctx.ones_row[:, :],
                              srow[0:1, 512:512 + w], start=True, stop=True))

    for kt in range(DT):
        @bk.vector
        def _(v, kt=kt):
            if kt == 0:
                v.wait_ge(sm["mm"].h, sm["mm"].n)
            v.tensor_sub(scr[:, 0:w], X[kt][:, xoff:xoff + w], psB[:, 0:w])
            ins = v.tensor_mul(XN[kt][:, noff:noff + w], scr[:, 0:w], psA[:, 0:w])
            if kt == DT - 1:
                sm["vec"].inc(ins)



HALO0 = 256
TH0 = TC + 2 * HALO0
NJT0 = TH0 // 128
NCH0 = TC // 512
VHS = 72


def build_l0_attn():
    ctx = Ctx(psum_banks=6)
    nc = ctx.nc
    sm = ctx.sems
    sb = ctx.sb

    xh = nc.dram_tensor("xh", [D, TH0], F32, kind="ExternalInput").ap()
    wqkvT = nc.dram_tensor("wqkvT", [D, 3 * D], F32, kind="ExternalInput").ap()
    woT = nc.dram_tensor("woT", [D, D], F32, kind="ExternalInput").ap()
    bqkvd = nc.dram_tensor("bqkv", [128, 3 * DT], F32, kind="ExternalInput").ap()
    bod = nc.dram_tensor("bo", [128, DT], F32, kind="ExternalInput").ap()
    maskd = nc.dram_tensor("masks", [128, 12 * 512], BF16, kind="ExternalInput").ap()
    identd = nc.dram_tensor("ident64d", [128, 64], BF16, kind="ExternalInput").ap()
    xo = nc.dram_tensor("xo", [D, TC], F32, kind="ExternalOutput").ap()

    XCb, XC = ctx.tiles("xc", DT, 512, F32)
    XNb, XN = ctx.tiles("xn", DT, TH0, BF16)
    Kb, KTl = ctx.tiles("kt", DT, TH0, BF16)
    Qb, QTc = ctx.tiles("qt", DT, 512, BF16)
    Ab, ATTc = ctx.tiles("at", DT, 512, BF16)
    OBb, OB = ctx.tiles("ob", 2, 512, F32)
    VH = sb("vh", [128, 16 * NJT0 * VHS], BF16)
    MSK = sb("msk", [128, 12 * 512], BF16)
    BQKV = sb("bqkv_t", [128, 3 * DT], F32)
    BO = sb("bo_t", [128, DT], F32)
    ident64 = sb("ident64", [128, 64], BF16)
    vtmp = sb("vtmp", [128, 512], BF16)
    scr = sb("scr", [128, 1024], F32)
    srow = sb("srow", [1, 2048], F32)
    mcol = sb("mcol", [128, 1], F32)
    pbuf = sb("pbuf", [128, 512], BF16)
    pmb = sb("pmb", [128, 512], BF16)
    linv = sb("linv", [1, 512], F32)
    linvb = sb("linvb", [64, 512], F32)
    wbuf = sb("wbuf", [128, 2 * D], BF16)

    psA, psB, psS, psO, psX, psM = ctx.ps
    pst = ctx.st.enter_context(nc.psum_tensor("pst", [128, 128], BF16)).ap()

    wfree = {}
    wready = {}
    ob_st = {0: 0, 1: 0}

    def emit_wload(wdram, col0, mt, seq):
        @bk.gpsimd
        def _(g, col0=col0, mt=mt, seq=seq):
            if seq >= 2:
                g.wait_ge(sm["mm"].h, wfree[seq - 2])
            g.dma_start(
                out=wbuf[:, (seq % 2) * D:(seq % 2) * D + D].rearrange(
                    "p (a q) -> p a q", a=DT),
                in_=wdram[:, col0 + mt * 128:col0 + (mt + 1) * 128].rearrange(
                    "(a p) m -> p a m", p=128),
            ).then_inc(sm["wl"].h, 16)
        sm["wl"].n += 16
        wready[seq] = sm["wl"].n
        return wbuf[:, (seq % 2) * D:(seq % 2) * D + D]

    with nc.Block() as bk:
        @bk.vector
        def _(v):
            v.memset(ctx.ones_col[:, :], 1.0)
            v.memset(ctx.ones_row[:, :], 1.0)
            sm["vec"].inc(v.memset(VH[:, :], 1.0))

        @bk.sync
        def _(s):
            s.dma_start(out=BQKV[:, :], in_=bqkvd).then_inc(sm["ld"].h, 16)
            s.dma_start(out=BO[:, :], in_=bod).then_inc(sm["ld"].h, 16)
            s.dma_start(out=MSK[:, :], in_=maskd).then_inc(sm["ld"].h, 16)
            s.dma_start(out=ident64[:, :], in_=identd).then_inc(sm["ld"].h, 16)
        sm["ld"].n += 64

        for c in range(TH0 // 512):
            t0 = c * 512

            @bk.sync
            def _(s, t0=t0):
                s.wait_ge(sm["vec"].h, sm["vec"].n)
                dma_tiles(s, sm["ld"], XCb, DT, xh[:, t0:t0 + 512])

            ld_now = sm["ld"].n

            @bk.tensor
            def _(t, ld_now=ld_now):
                t.wait_ge(sm["ld"].h, ld_now)

            @bk.scalar
            def _(sc, ld_now=ld_now):
                sc.wait_ge(sm["ld"].h, ld_now)

            emit_ln_chunk(ctx, bk, XC, XN, 0, t0, 512, psA, psB, scr, srow)

        ln_done_vec = sm["vec"].n

        seq = 0
        for name, wcol0, bt0 in (("K", D, DT), ("V", 2 * D, 2 * DT)):
            for mt in range(DT):
                wv = emit_wload(wqkvT, wcol0, mt, seq)

                for c in range(TH0 // 512):
                    t0 = c * 512

                    @bk.tensor
                    def _(t, seq=seq, c=c, t0=t0, wv=wv):
                        if c == 0:
                            t.wait_ge(sm["wl"].h, wready[seq])
                            if seq == 0:
                                t.wait_ge(sm["vec"].h, ln_done_vec)
                        t.wait_ge(sm["vec"].h, sm["vec"].n)
                        for kt in range(DT):
                            sm["mm"].inc(t.matmul(psS[:, 0:512], wv[:, kt * 128:(kt + 1) * 128],
                                     XN[kt][:, t0:t0 + 512],
                                     start=(kt == 0), stop=(kt == DT - 1)))

                    if name == "K":
                        @bk.vector
                        def _(v, mt=mt, t0=t0, bt0=bt0):
                            v.wait_ge(sm["mm"].h, sm["mm"].n)
                            sm["vec"].inc(v.tensor_scalar_add(KTl[mt][:, t0:t0 + 512], psS[:, 0:512],
                                                BQKV[:, bt0 + mt:bt0 + mt + 1]))
                    else:
                        @bk.vector
                        def _(v, mt=mt, bt0=bt0):
                            v.wait_ge(sm["mm"].h, sm["mm"].n)
                            sm["vec"].inc(v.tensor_scalar_add(vtmp[:, 0:512], psS[:, 0:512],
                                                BQKV[:, bt0 + mt:bt0 + mt + 1]))

                        for sub in range(2):
                            for jt in range(4):
                                @bk.tensor
                                def _(t, sub=sub, jt=jt):
                                    t.wait_ge(sm["vec"].h, sm["vec"].n)
                                    sm["mm"].inc(t.transpose(pst[:, 0:64],
                                                vtmp[64 * sub:64 * sub + 64,
                                                     jt * 128:(jt + 1) * 128],
                                                ident64[64 * sub:64 * sub + 64, :]))

                                @bk.vector
                                def _(v, mt=mt, c=c, sub=sub, jt=jt):
                                    v.wait_ge(sm["mm"].h, sm["mm"].n)
                                    idx = ((2 * mt + sub) * NJT0 + 4 * c + jt) * VHS
                                    sm["vec"].inc(v.tensor_copy(VH[:, idx:idx + 64], pst[:, 0:64]))

                wfree[seq] = sm["mm"].n
                seq += 1

        for c in range(NCH0):
            q0 = HALO0 + c * 512

            for mt in range(DT):
                wv = emit_wload(wqkvT, 0, mt, seq)

                @bk.tensor
                def _(t, seq=seq, q0=q0, wv=wv):
                    t.wait_ge(sm["wl"].h, wready[seq])
                    t.wait_ge(sm["vec"].h, sm["vec"].n)
                    for kt in range(DT):
                        sm["mm"].inc(t.matmul(psS[:, 0:512], wv[:, kt * 128:(kt + 1) * 128],
                                 XN[kt][:, q0:q0 + 512],
                                 start=(kt == 0), stop=(kt == DT - 1)))

                @bk.vector
                def _(v, mt=mt):
                    v.wait_ge(sm["mm"].h, sm["mm"].n)
                    v.wait_ge(sm["sc"].h, sm["sc"].n)
                    sm["vec"].inc(v.tensor_scalar_add(QTc[mt][:, 0:512], psS[:, 0:512],
                                        BQKV[:, mt:mt + 1]))

                wfree[seq] = sm["mm"].n
                seq += 1

            q_done_vec = sm["vec"].n

            for h in range(16):
                ht, hr = h // 2, 64 * (h % 2)
                k0 = c * 512

                @bk.scalar
                def _(sc, ht=ht, hr=hr, k0=k0):
                    sc.wait_ge(sm["vec"].h, sm["vec"].n)
                    sc.activation(scr[0:64, 0:512], QTc[ht][hr:hr + 64, 0:512],
                                  AF.Square)
                    sc.activation(scr[0:64, 512:1024],
                                  KTl[ht][hr:hr + 64, k0:k0 + 512], AF.Square)
                    sm["sc"].inc(sc.activation(scr[64:128, 0:512],
                                  KTl[ht][hr:hr + 64, k0 + 512:k0 + 1024], AF.Square))

                @bk.tensor
                def _(t):
                    t.wait_ge(sm["sc"].h, sm["sc"].n)
                    t.wait_ge(sm["vec"].h, sm["vec"].n)
                    t.matmul(psM[0:1, 0:512], ctx.ones_col[0:64, :],
                             scr[0:64, 0:512], start=True, stop=True)
                    t.matmul(psM[32:33, 0:512], ctx.ones_col[0:64, :],
                             scr[0:64, 512:1024], start=True, stop=True)
                    sm["mm"].inc(t.matmul(psM[64:65, 0:512], ctx.ones_col[64:128, :],
                             scr[64:128, 0:512], start=True, stop=True))

                @bk.vector
                def _(v):
                    v.wait_ge(sm["mm"].h, sm["mm"].n)
                    v.reduce_max(srow[0:1, 0:1], psM[0:1, 0:512], axis=AX.X)
                    v.reduce_max(srow[0:1, 1:2], psM[32:33, 0:512], axis=AX.X)
                    v.reduce_max(srow[0:1, 2:3], psM[64:65, 0:512], axis=AX.X)
                    v.tensor_max(srow[0:1, 1:2], srow[0:1, 1:2], srow[0:1, 2:3])
                    sm["vec"].inc(v.tensor_mul(srow[0:1, 0:1], srow[0:1, 0:1], srow[0:1, 1:2]))

                @bk.scalar
                def _(sc):
                    sc.wait_ge(sm["vec"].h, sm["vec"].n)
                    sc.activation(srow[0:1, 0:1], srow[0:1, 0:1], AF.Ln)
                    sc.activation(srow[0:1, 0:1], srow[0:1, 0:1], AF.Exp, scale=0.5)
                    sm["sc"].inc(sc.mul(srow[0:1, 1:2], srow[0:1, 0:1], -0.125))

                @bk.tensor
                def _(t):
                    t.wait_ge(sm["sc"].h, sm["sc"].n)
                    sm["mm"].inc(t.matmul(psM[0:128, 0:1], ctx.ones_row[:, :], srow[0:1, 1:2],
                             start=True, stop=True))

                @bk.vector
                def _(v):
                    v.wait_ge(sm["mm"].h, sm["mm"].n)
                    sm["vec"].inc(v.tensor_copy(mcol[:, :], psM[0:128, 0:1]))

                mcol_vec = sm["vec"].n

                if c == 0:
                    mrel = [8, 9, 2, 3, 4, 5, 6, 7]
                elif c == NCH0 - 1:
                    mrel = [0, 1, 2, 3, 4, 5, 10, 11]
                else:
                    mrel = [0, 1, 2, 3, 4, 5, 6, 7]

                for kt in range(8):
                    @bk.tensor
                    def _(t, ht=ht, hr=hr, kt=kt, k0=k0, mcol_vec=mcol_vec):
                        if kt == 0:
                            t.wait_ge(sm["vec"].h, mcol_vec)
                        t.wait_ge(sm["sc"].h, sm["sc"].n)
                        sm["mm"].inc(t.matmul(psS[:, 0:512],
                                 KTl[ht][hr:hr + 64, k0 + kt * 128:k0 + (kt + 1) * 128],
                                 QTc[ht][hr:hr + 64, 0:512],
                                 start=True, stop=True))

                    @bk.scalar
                    def _(sc):
                        sc.wait_ge(sm["mm"].h, sm["mm"].n)
                        sc.wait_ge(sm["vec"].h, sm["vec"].n)
                        sm["sc"].inc(sc.activation(pbuf[:, 0:512], psS[:, 0:512], AF.Exp,
                                      scale=0.125, bias=mcol[:, 0:1]))

                    @bk.vector
                    def _(v, kt=kt, mrel=mrel):
                        v.wait_ge(sm["sc"].h, sm["sc"].n)
                        mi = mrel[kt]
                        sm["vec"].inc(v.tensor_mul(pmb[:, 0:512], pbuf[:, 0:512],
                                     MSK[:, mi * 512:(mi + 1) * 512]))

                    @bk.tensor
                    def _(t, h=h, c=c, kt=kt):
                        t.wait_ge(sm["vec"].h, sm["vec"].n)
                        idx = (h * NJT0 + 4 * c + kt) * VHS
                        ins = t.matmul(psO[0:65, 0:512], VH[:, idx:idx + 65],
                                       pmb[:, 0:512], start=(kt == 0), stop=(kt == 7))
                        if kt == 7:
                            sm["mm"].inc(ins)

                @bk.vector
                def _(v):
                    v.wait_ge(sm["mm"].h, sm["mm"].n)
                    v.tensor_scalar_add(linv[0:1, 0:512], psO[64:65, 0:512], 1e-30)
                    sm["vec"].inc(v.reciprocal(linv[0:1, 0:512], linv[0:1, 0:512]))

                @bk.tensor
                def _(t):
                    t.wait_ge(sm["vec"].h, sm["vec"].n)
                    sm["mm"].inc(t.matmul(psM[0:64, 0:512], ctx.ones_row[0:1, 0:64],
                                          linv[0:1, 0:512], start=True, stop=True))

                @bk.vector
                def _(v, ht=ht, hr=hr):
                    v.wait_ge(sm["mm"].h, sm["mm"].n)
                    v.tensor_copy(linvb[:, 0:512], psM[0:64, 0:512])
                    sm["vec"].inc(v.tensor_mul(ATTc[ht][hr:hr + 64, 0:512], psO[0:64, 0:512],
                                 linvb[:, 0:512]))

            att_done_vec = sm["vec"].n

            @bk.sync
            def _(s, c=c):
                s.wait_ge(sm["vec"].h, ln_done_vec)
                s.wait_ge(sm["v2"].h, sm["v2"].n)
                dma_tiles(s, sm["ld"], XCb, DT, xh[:, HALO0 + c * 512:HALO0 + c * 512 + 512])

            xr_ld = sm["ld"].n

            for mt in range(DT):
                wv = emit_wload(woT, 0, mt, seq)

                @bk.tensor
                def _(t, seq=seq, wv=wv, att_done_vec=att_done_vec):
                    t.wait_ge(sm["wl"].h, wready[seq])
                    t.wait_ge(sm["vec"].h, att_done_vec)
                    t.wait_ge(sm["v2"].h, sm["v2"].n)
                    for kt in range(DT):
                        sm["mm"].inc(t.matmul(psX[:, 0:512], wv[:, kt * 128:(kt + 1) * 128],
                                 ATTc[kt][:, 0:512],
                                 start=(kt == 0), stop=(kt == DT - 1)))

                ob = OB[mt % 2]

                @bk.vector
                def _(v, mt=mt, ob=ob, xr_ld=xr_ld):
                    v.wait_ge(sm["mm"].h, sm["mm"].n)
                    v.wait_ge(sm["ld"].h, xr_ld)
                    v.wait_ge(sm["st"].h, ob_st[mt % 2])
                    sm["v2"].inc(v.scalar_tensor_tensor(ob[:, 0:512], psX[:, 0:512],
                                           BO[:, mt:mt + 1], XC[mt][:, 0:512],
                                           ALU.add, ALU.add))

                v2_now = sm["v2"].n

                @bk.sync
                def _(s, mt=mt, c=c, ob=ob, v2_now=v2_now):
                    s.wait_ge(sm["v2"].h, v2_now)
                    s.dma_start(out=xo[mt * 128:(mt + 1) * 128, c * 512:c * 512 + 512],
                                in_=ob[:, 0:512]).then_inc(sm["st"].h, 16)
                sm["st"].n += 16
                ob_st[mt % 2] = sm["st"].n

                wfree[seq] = sm["mm"].n
                seq += 1

        @bk.sync
        def _(s):
            s.wait_ge(sm["st"].h, sm["st"].n)

    ctx.close()
    return nc


def host_l0_attn_inputs(x, wq, bq, wk, bk_, wv, bv, wo, bo):
    ident = np.concatenate([np.eye(64), np.eye(64)], axis=0).astype(BF)
    wqkvT = np.ascontiguousarray(
        np.concatenate([wq.T, wk.T, wv.T], axis=1)).astype(np.float32)
    woT = np.ascontiguousarray(wo.T).astype(np.float32)
    bqkv = np.ascontiguousarray(
        np.concatenate([bq, bk_, bv]).reshape(3 * DT, 128).T).astype(np.float32)
    boh = np.ascontiguousarray(bo.reshape(DT, 128).T).astype(np.float32)

    maps = []
    jj = np.arange(128)
    ii = np.arange(512)
    for core in range(NCORES):
        b, q = divmod(core, 4)
        t0 = q * TC
        xhp = np.zeros((TH0, D), np.float32)
        lo, hi = t0 - HALO0, t0 + TC + HALO0
        slo, shi = max(lo, 0), min(hi, S)
        xhp[slo - lo:shi - lo] = x[b, slo:shi]
        xhT = np.ascontiguousarray(xhp.T)

        masks = np.zeros((128, 12 * 512), np.float32)
        for kt in range(8):
            delta = ii[None, :] - (kt * 128 + jj[:, None] - 256)
            masks[:, kt * 512:(kt + 1) * 512] = (np.abs(delta) <= 256)
        for slot, (cc, kt) in ((8, (0, 0)), (9, (0, 1)), (10, (3, 6)), (11, (3, 7))):
            delta = ii[None, :] - (kt * 128 + jj[:, None] - 256)
            gj = t0 - 256 + cc * 512 + kt * 128 + jj[:, None] + 0 * ii[None, :]
            masks[:, slot * 512:(slot + 1) * 512] = (
                (np.abs(delta) <= 256) & (gj >= 0) & (gj < S))

        maps.append({
            "xh": xhT, "wqkvT": wqkvT, "woT": woT, "bqkv": bqkv, "bo": boh,
            "masks": masks.astype(BF), "ident64d": ident,
        })
    return maps


def run_l0_attn(x, wq, bq, wk, bk_, wv, bv, wo, bo, nc=None):
    nc = nc or build_l0_attn()
    maps = host_l0_attn_inputs(x, wq, bq, wk, bk_, wv, bv, wo, bo)
    res = run_bass_kernel_spmd(nc, maps, core_ids=list(range(NCORES)))
    out = np.empty((B, S, D), np.float32)
    for core in range(NCORES):
        b, q = divmod(core, 4)
        out[b, q * TC:(q + 1) * TC] = res.results[core]["xo"].T
    return out


_CACHE = {}


def _prog(name, builder):
    if name not in _CACHE:
        _CACHE[name] = builder()
    return _CACHE[name]


def kernel(**inputs):
    g = lambda k: np.asarray(inputs[k])
    x = g("L0_attn_x").astype(np.float32)

    x = run_l0_attn(x, g("L0_attn_wq"), g("L0_attn_bq"), g("L0_attn_wk"),
                    g("L0_attn_bk"), g("L0_attn_wv"), g("L0_attn_bv"),
                    g("L0_attn_wo"), g("L0_attn_bo"),
                    nc=_prog("l0a", build_l0_attn))
    x = run_l0_ffn(x, g("L0_ffn_weight"), g("L0_ffn_alpha"), g("L0_ffn_bias"),
                   nc=_prog("l0f", build_l0_ffn))
    x = run_l1_attn(x, g("L1_attn_Wq"), g("L1_attn_Wk"), g("L1_attn_Wv"),
                    g("L1_attn_Wo"), g("L1_attn_bo"),
                    nc=_prog("l1a", build_l1_attn))
    w1f = {k: g("L1_ffn_" + k) for k in
           ["Wqkv", "q_scale", "k_scale", "Wo", "ln_g", "ln_b",
            "W1", "b1", "W2", "b2"]}
    z = run_lnp(x, nc=_prog("lnp", build_lnp))
    z = run_l1f(z, w1f, nc=_prog("l1f", build_l1f))
    x = run_addr(x, z, nc=_prog("addr", build_addr))
    w2a = {k: g("L2_attn_" + k) for k in
           ["W_in", "conv_w", "conv_b", "W_x", "W_dt", "b_dt",
            "A_log", "D", "W_out"]}
    resa = run_l2a(x, w2a, nc=_prog("l2a", build_l2a))
    resb = run_l2b(resa, w2a, nc=_prog("l2b", build_l2b))
    x = run_l2c(resa, resb, x, w2a, nc=_prog("l2c", build_l2c))
    x = run_moe("l2", x, g("L2_ffn_gate_w"), g("L2_ffn_w1"), g("L2_ffn_w2"),
                g("L2_ffn_w3"), nc=_prog("moe2", lambda: build_moe("l2")))
    x = run_l3_attn(x, g("L3_attn_Wqkv"), g("L3_attn_bqkv"), g("L3_attn_Wo"),
                    g("L3_attn_bo"), nc=_prog("l3a", build_l3_attn))
    x = run_moe("l3", x, g("L3_ffn_gate_w"), g("L3_ffn_expert_w1"),
                g("L3_ffn_expert_w2"), ln_g=g("L3_ffn_ln_gamma"),
                ln_b=g("L3_ffn_ln_beta"), nc=_prog("moe3", lambda: build_moe("l3")))
    return x



def build_l0_ffn(debug=False):
    ctx = Ctx(psum_banks=5)
    nc = ctx.nc
    sm = ctx.sems
    sb = ctx.sb

    xd = nc.dram_tensor("xd", [D, TC], F32, kind="ExternalInput").ap()
    wT = nc.dram_tensor("wT", [D, D], F32, kind="ExternalInput").ap()
    biasd = nc.dram_tensor("bias", [128, DT], F32, kind="ExternalInput").ap()
    alphad = nc.dram_tensor("alpha", [128, 1], F32, kind="ExternalInput").ap()
    xo = nc.dram_tensor("xo", [D, TC], F32, kind="ExternalOutput").ap()
    colscratch = nc.dram_tensor("colscratch", [128, 1], F32, kind="Internal").ap()
    dcolscratch = nc.dram_tensor("dcolscratch", [128, 2], F32, kind="Internal").ap()

    Xb, X = ctx.tiles("x", DT, TC, F32)
    XNb, XN = ctx.tiles("xn", DT, TC, BF16)
    OBb, OB = ctx.tiles("ob", 2, 512, F32)
    BIAS = sb("bias_t", [128, DT], F32)
    ALPHA = sb("alpha_t", [128, 1], F32)
    scr = sb("scr", [128, 1024], F32)
    srow = sb("srow", [1, 2048], F32)
    dcol = sb("dcol", [128, 2], F32)
    wf = sb("wf", [128, 2 * D], F32)
    wt3 = sb("wt3", [128, D], BF16)
    wtf = sb("wtf", [128, D], F32)
    wsgn = sb("wsgn", [128, D], BF16)
    wm = sb("wm", [128, D], BF16)
    pcol = sb("pcol", [128, 1], F32)
    ones128 = sb("ones128", [128, 128], F32)
    wsum = sb("wsum", [128, 144], F32)

    psA, psB, psS, psX, psM = ctx.ps

    with nc.Block() as bk:
        @bk.vector
        def _(v):
            v.memset(ctx.ones_col[:, :], 1.0)
            v.memset(ctx.ones_row[:, :], 1.0)
            v.memset(ones128[:, :], 1.0)
            v.memset(ctx.flushbuf[:, :], 1.0)
            sm["vec"].inc(v.memset(ctx.ones_row_bf[:, :], 1.0))

        @bk.sync
        def _(s):
            s.dma_start(out=BIAS[:, :], in_=biasd).then_inc(sm["ld"].h, 16)
            s.dma_start(out=ALPHA[:, :], in_=alphad).then_inc(sm["ld"].h, 16)
            dma_tiles(s, sm["ld"], Xb, DT, xd)
        sm["ld"].n += 32

        x_ld = sm["ld"].n

        @bk.tensor
        def _(t):
            t.wait_ge(sm["ld"].h, x_ld)

        @bk.scalar
        def _(sc):
            sc.wait_ge(sm["ld"].h, x_ld)

        @bk.vector
        def _(v):
            v.wait_ge(sm["ld"].h, x_ld)

        for c in range(TC // 512):
            emit_ln_chunk(ctx, bk, X, XN, c * 512, c * 512, 512, psA, psB, scr, srow)
        ln_done = sm["vec"].n

        for ct in range(DT):
            @bk.gpsimd
            def _(g, ct=ct):
                if ct >= 2:
                    g.wait_ge(sm["vec"].h, sm[f"_wabs{ct - 2}"])
                g.dma_start(
                    out=wf[:, (ct % 2) * D:(ct % 2) * D + D].rearrange(
                        "p (a q) -> p a q", a=DT),
                    in_=wT[:, ct * 128:(ct + 1) * 128].rearrange(
                        "(a p) m -> p a m", p=128),
                ).then_inc(sm["wl"].h, 16)
            sm["wl"].n += 16
            wl_now = sm["wl"].n

            @bk.vector
            def _(v, ct=ct, wl_now=wl_now):
                v.wait_ge(sm["wl"].h, wl_now)
                for i in range(DT):
                    ins = v.tensor_reduce(wsum[:, ct * 8 + i:ct * 8 + i + 1],
                                    wf[:, (ct % 2) * D + i * 128:(ct % 2) * D + (i + 1) * 128],
                                    AX.X, ALU.add, apply_absolute_value=True)
                sm["vec"].inc(ins)
            sm[f"_wabs{ct}"] = sm["vec"].n

        @bk.vector
        def _(v):
            v.wait_ge(sm["vec"].h, sm["vec"].n)
            sm["vec"].inc(v.tensor_reduce(wsum[:, 64:65], wsum[:, 0:64], AX.X, ALU.add))

        vv = sm["vec"].n

        @bk.sync
        def _(s, vv=vv):
            s.wait_ge(sm["vec"].h, vv)
            s.dma_start(out=colscratch, in_=wsum[:, 64:65]).then_inc(sm["ld"].h, 16)
        sm["ld"].n += 16
        ldv1 = sm["ld"].n

        @bk.sync
        def _(s, ldv1=ldv1):
            s.wait_ge(sm["ld"].h, ldv1)
            s.dma_start(out=pcol[:, :], in_=colscratch).then_inc(sm["ld"].h, 16)
        sm["ld"].n += 16
        ldv2 = sm["ld"].n

        @bk.tensor
        def _(t, ldv2=ldv2):
            t.wait_ge(sm["ld"].h, ldv2)
            t.wait_ge(sm["vec"].h, sm["vec"].n)
            sm["mm"].inc(t.matmul(psM[0:1, 0:128], pcol[:, 0:1], ones128[:, :],
                                  start=True, stop=True))

        @bk.vector
        def _(v):
            v.wait_ge(sm["mm"].h, sm["mm"].n)
            sm["vec"].inc(v.tensor_scalar_mul(srow[0:1, 0:128], psM[0:1, 0:128],
                                              0.7 / (D * D)))

        @bk.tensor
        def _(t):
            t.wait_ge(sm["vec"].h, sm["vec"].n)
            sm["mm"].inc(t.matmul(psM[0:128, 0:128], ctx.ones_row[:, :],
                                  srow[0:1, 0:128], start=True, stop=True))

        @bk.vector
        def _(v):
            v.wait_ge(sm["mm"].h, sm["mm"].n)
            sm["vec"].inc(v.tensor_copy(dcol[:, 0:2], psM[0:128, 0:2]))

        dv = sm["vec"].n

        @bk.sync
        def _(s, dv=dv):
            s.wait_ge(sm["vec"].h, dv)
            s.dma_start(out=dcolscratch, in_=dcol[:, 0:2]).then_inc(sm["ld"].h, 16)
        sm["ld"].n += 16
        ldd = sm["ld"].n

        @bk.sync
        def _(s, ldd=ldd):
            s.wait_ge(sm["ld"].h, ldd)
            s.dma_start(out=dcol[:, 0:2], in_=dcolscratch).then_inc(sm["ld"].h, 16)
        sm["ld"].n += 16
        dcol_ld = sm["ld"].n
        dcol_vec = sm["vec"].n

        ob_st = {0: 0, 1: 0}
        for mt in range(DT):
            @bk.gpsimd
            def _(g, mt=mt):
                if mt >= 2:
                    g.wait_ge(sm["vec"].h, sm[f"_wtern{mt - 2}"])
                else:
                    g.wait_ge(sm["vec"].h, sm[f"_wabs{6 + (mt % 2)}"])
                g.dma_start(
                    out=wf[:, (mt % 2) * D:(mt % 2) * D + D].rearrange(
                        "p (a q) -> p a q", a=DT),
                    in_=wT[:, mt * 128:(mt + 1) * 128].rearrange(
                        "(a p) m -> p a m", p=128),
                ).then_inc(sm["wl"].h, 16)
            sm["wl"].n += 16
            wl_now = sm["wl"].n

            @bk.scalar
            def _(sc, mt=mt, wl_now=wl_now):
                sc.wait_ge(sm["wl"].h, wl_now)
                sc.wait_ge(sm["vec"].h, sm["vec"].n)
                wv = wf[:, (mt % 2) * D:(mt % 2) * D + D]
                sc.activation(wtf[:, 0:D], wv[:, 0:D], AF.Abs)
                sm["sc"].inc(sc.activation(wsgn[:, 0:D], wv[:, 0:D], AF.Sign))

            @bk.vector
            def _(v, mt=mt):
                v.wait_ge(sm["sc"].h, sm["sc"].n)
                v.wait_ge(sm["ld"].h, dcol_ld)
                v.wait_ge(sm["mm"].h, sm["mm"].n)
                v.tensor_single_scalar(wm[:, 0:D], wtf[:, 0:D],
                                       dcol[:, 0:1], ALU.is_gt)
                sm["vec"].inc(v.tensor_mul(wt3[:, 0:D], wm[:, 0:D], wsgn[:, 0:D]))
            sm[f"_wtern{mt}"] = sm["vec"].n
            tern_vec = sm["vec"].n

            for c in range(TC // 512):
                @bk.tensor
                def _(t, mt=mt, c=c, tern_vec=tern_vec):
                    if c == 0:
                        t.wait_ge(sm["vec"].h, tern_vec)
                        if mt == 0:
                            t.wait_ge(sm["vec"].h, ln_done)
                    t.wait_ge(sm["v2"].h, sm["v2"].n)
                    for kt in range(DT):
                        sm["mm"].inc(t.matmul(psX[:, 0:512],
                                              wt3[:, kt * 128:(kt + 1) * 128],
                                              XN[kt][:, c * 512:(c + 1) * 512],
                                              start=(kt == 0), stop=(kt == DT - 1)))

                ob = OB[c % 2]

                @bk.vector
                def _(v, mt=mt, c=c, ob=ob):
                    v.wait_ge(sm["ld"].h, 32)
                    v.wait_ge(sm["mm"].h, sm["mm"].n)
                    v.wait_ge(sm["st"].h, ob_st[c % 2])
                    v.scalar_tensor_tensor(ob[:, 0:512], psX[:, 0:512],
                                           ALPHA[:, 0:1], X[mt][:, c * 512:(c + 1) * 512],
                                           ALU.mult, ALU.add)
                    sm["v2"].inc(v.tensor_scalar_add(ob[:, 0:512], ob[:, 0:512],
                                                     BIAS[:, mt:mt + 1]))
                v2_now = sm["v2"].n

                @bk.sync
                def _(s, mt=mt, c=c, ob=ob, v2_now=v2_now):
                    s.wait_ge(sm["v2"].h, v2_now)
                    s.dma_start(out=xo[mt * 128:(mt + 1) * 128, c * 512:(c + 1) * 512],
                                in_=ob[:, 0:512]).then_inc(sm["st"].h, 16)
                sm["st"].n += 16
                ob_st[c % 2] = sm["st"].n

        if debug:
            dbg = nc.dram_tensor("dbg", [128, 2048 + 160 + 1024], F32,
                                 kind="ExternalOutput").ap()

            @bk.gpsimd
            def _(g):
                g.wait_ge(sm["st"].h, sm["st"].n)
                g.dma_start(out=dbg[:, 0:2048], in_=XNb[:, 0:2048]).then_inc(sm["wl"].h, 16)
                g.dma_start(out=dbg[:, 2048:2192], in_=wsum[:, 0:144]).then_inc(sm["wl"].h, 16)
                g.dma_start(out=dbg[:, 2192:2194], in_=dcol[:, 0:2]).then_inc(sm["wl"].h, 16)
                g.dma_start(out=dbg[:, 2208:3232], in_=wt3[:, 0:1024]).then_inc(sm["wl"].h, 16)
                g.wait_ge(sm["wl"].h, sm["wl"].n + 64)
            sm["wl"].n += 64

        @bk.sync
        def _(s):
            s.wait_ge(sm["st"].h, sm["st"].n)

    ctx.close()
    return nc


def host_l0_ffn_inputs(x, weight, alpha, bias):
    wT = np.ascontiguousarray(weight.T).astype(np.float32)
    bh = np.ascontiguousarray(bias.reshape(DT, 128).T).astype(np.float32)
    al = np.broadcast_to(np.asarray(alpha, np.float32).reshape(1, 1), (128, 1)).copy()
    maps = []
    for core in range(NCORES):
        b, q = divmod(core, 4)
        xT = np.ascontiguousarray(x[b, q * TC:(q + 1) * TC].T).astype(np.float32)
        maps.append({"xd": xT, "wT": wT, "bias": bh, "alpha": al})
    return maps


def run_l0_ffn(x, weight, alpha, bias, nc=None):
    nc = nc or build_l0_ffn()
    maps = host_l0_ffn_inputs(x, weight, alpha, bias)
    res = run_bass_kernel_spmd(nc, maps, core_ids=list(range(NCORES)))
    out = np.empty((B, S, D), np.float32)
    for core in range(NCORES):
        b, q = divmod(core, 4)
        out[b, q * TC:(q + 1) * TC] = res.results[core]["xo"].T
    return out



def emit_ln_gb(ctx, bk, X, OUT, xoff, w, psA, psB, scr, srow, gcol, bcol):
    sm = ctx.sems

    @bk.tensor
    def _(t):
        t.wait_ge(sm["vec"].h, sm["vec"].n)
        for kt in range(DT):
            sm["mm"].inc(t.matmul(psA[0:1, 0:w], ctx.ones_col[:, :],
                                  X[kt][:, xoff:xoff + w],
                                  start=(kt == 0), stop=(kt == DT - 1)))

    for kt in range(DT):
        @bk.scalar
        def _(sc, kt=kt):
            sc.wait_ge(sm["vec"].h, sm["vec"].n)
            sc.wait_ge(sm["mm"].h, sm["mm"].n)
            sm["sc"].inc(sc.activation(scr[:, 0:w], X[kt][:, xoff:xoff + w], AF.Square))

        @bk.tensor
        def _(t, kt=kt):
            t.wait_ge(sm["sc"].h, sm["sc"].n)
            sm["mm"].inc(t.matmul(psA[32:33, 0:w], ctx.ones_col[:, :], scr[:, 0:w],
                                  start=(kt == 0), stop=(kt == DT - 1)))

    @bk.vector
    def _(v):
        v.wait_ge(sm["mm"].h, sm["mm"].n)
        v.tensor_scalar_mul(srow[0:1, 0:w], psA[0:1, 0:w], 1.0 / D)
        v.tensor_scalar_mul(srow[0:1, 512:512 + w], psA[32:33, 0:w], 1.0 / D)
        v.tensor_mul(srow[0:1, 1024:1024 + w], srow[0:1, 0:w], srow[0:1, 0:w])
        v.tensor_sub(srow[0:1, 512:512 + w], srow[0:1, 512:512 + w],
                     srow[0:1, 1024:1024 + w])
        sm["vec"].inc(v.tensor_scalar_add(srow[0:1, 512:512 + w],
                                          srow[0:1, 512:512 + w], LN_EPS))

    @bk.scalar
    def _(sc):
        sc.wait_ge(sm["vec"].h, sm["vec"].n)
        sc.activation(srow[0:1, 512:512 + w], srow[0:1, 512:512 + w], AF.Ln)
        sm["sc"].inc(sc.activation(srow[0:1, 512:512 + w], srow[0:1, 512:512 + w],
                                   AF.Exp, scale=-0.5))

    @bk.tensor
    def _(t):
        t.wait_ge(sm["sc"].h, sm["sc"].n)
        t.matmul(psB[:, 0:w], ctx.ones_row[:, :], srow[0:1, 0:w],
                 start=True, stop=True)
        sm["mm"].inc(t.matmul(psA[:, 0:w], ctx.ones_row[:, :],
                              srow[0:1, 512:512 + w], start=True, stop=True))

    for kt in range(DT):
        @bk.vector
        def _(v, kt=kt):
            if kt == 0:
                v.wait_ge(sm["mm"].h, sm["mm"].n)
            v.tensor_sub(scr[:, 0:w], X[kt][:, xoff:xoff + w], psB[:, 0:w])
            v.tensor_mul(scr[:, 0:w], scr[:, 0:w], psA[:, 0:w])
            ins = v.tensor_scalar(OUT[kt][:, xoff:xoff + w], scr[:, 0:w],
                                  gcol[:, kt:kt + 1], bcol[:, kt:kt + 1],
                                  ALU.mult, ALU.add)
            if kt == DT - 1:
                sm["vec"].inc(ins)


def build_moe(kind, debug=False):
    NE = 8
    H = 4096 if kind == "l2" else 2048
    MT = H // 128
    ctx = Ctx(psum_banks=6)
    nc = ctx.nc
    sm = ctx.sems
    sb = ctx.sb

    xd = nc.dram_tensor("xd", [D, TC], F32, kind="ExternalInput").ap()
    gwT = nc.dram_tensor("gwT", [D, 8], F32, kind="ExternalInput").ap()
    w1T = nc.dram_tensor("w1T", [D, NE * H], F32, kind="ExternalInput").ap()
    w2T = nc.dram_tensor("w2T", [NE * H, D], F32, kind="ExternalInput").ap()
    if kind == "l2":
        w3T = nc.dram_tensor("w3T", [D, NE * H], F32, kind="ExternalInput").ap()
    else:
        gld = nc.dram_tensor("gl", [128, DT], F32, kind="ExternalInput").ap()
        bld = nc.dram_tensor("bl", [128, DT], F32, kind="ExternalInput").ap()
    sel8d = nc.dram_tensor("sel8", [8, 8 * 128], F32, kind="ExternalInput").ap()
    ident128d = nc.dram_tensor("ident128", [128, 128], F32, kind="ExternalInput").ap()
    rtd = nc.dram_tensor("rtd", [128, 128], F32, kind="Internal").ap()
    xo = nc.dram_tensor("xo", [D, TC], F32, kind="ExternalOutput").ap()

    NCH = TC // 512
    Xb, XC = ctx.tiles("xc", DT, 512, F32)
    XNb, XN = ctx.tiles("xn", DT, TC, BF16)
    ACCb, ACC = ctx.tiles("acc", DT, TC, F32)
    OBb, OB = ctx.tiles("ob", 2, 512, F32)
    GW = sb("gw", [128, DT * 8], BF16)
    SEL8 = sb("sel8_t", [8, 8 * 128], F32)
    ID128 = sb("id128", [128, 128], F32)
    CQ = sb("cq", [128, NCH * 32], F32)
    CROW = sb("crow", [8, NCH * 512], F32)
    RT = sb("rt", [128, 128], F32)
    RTw = sb("rtw", [128, 128], F32)
    S2 = sb("s2", [128, 32], F32)
    scr = sb("scr", [128, 1024], F32)
    srow = sb("srow", [1, 2048], F32)
    s1sb = sb("s1sb", [128, 512], BF16)
    hgsb = sb("hgsb", [128, 512], BF16)
    HG = sb("hg", [128, MT * 512], BF16)
    wbuf = sb("wbuf", [128, 2 * D], BF16)
    wbuf2 = sb("wbuf2", [128, 2 * H], BF16)
    if kind == "l3":
        GL = sb("gl_t", [128, DT], F32)
        BL = sb("bl_t", [128, DT], F32)

    psG, psCb, psH1, psH3, psA, psB = ctx.ps
    psY = psH1
    psY = psA
    psT = psB

    wready = {}
    wfree = {}

    with nc.Block() as bk:
        @bk.vector
        def _(v):
            v.memset(ctx.ones_col[:, :], 1.0)
            v.memset(ctx.ones_row[:, :], 1.0)
            v.memset(ctx.flushbuf[:, :], 1.0)
            sm["vec"].inc(v.memset(ctx.ones_row_bf[:, :], 1.0))

        @bk.sync
        def _(s):
            s.dma_start(out=SEL8[:, :], in_=sel8d).then_inc(sm["ld"].h, 16)
            s.dma_start(out=ID128[:, :], in_=ident128d).then_inc(sm["ld"].h, 16)
            if kind == "l3":
                s.dma_start(out=GL[:, :], in_=gld).then_inc(sm["ld"].h, 16)
                s.dma_start(out=BL[:, :], in_=bld).then_inc(sm["ld"].h, 16)
        sm["ld"].n += 32 if kind == "l2" else 64

        @bk.gpsimd
        def _(g):
            g.dma_start(out=GW[:, :].rearrange("p (a e) -> p a e", a=DT),
                        in_=gwT.rearrange("(a p) e -> p a e", p=128)
                        ).then_inc(sm["wl"].h, 16)
        sm["wl"].n += 16
        gw_wl = sm["wl"].n

        for c in range(NCH):
            t0 = c * 512

            @bk.sync
            def _(s, t0=t0):
                s.wait_ge(sm["vec"].h, sm["vec"].n)
                dma_tiles(s, sm["ld"], Xb, DT, xd[:, t0:t0 + 512])

            ld_now = sm["ld"].n

            @bk.tensor
            def _(t, ld_now=ld_now):
                t.wait_ge(sm["ld"].h, ld_now)

            @bk.scalar
            def _(sc, ld_now=ld_now):
                sc.wait_ge(sm["ld"].h, ld_now)

            emit_ln_chunk(ctx, bk, XC, XN, 0, t0, 512, psA, psB, scr, srow)

        ln_done = sm["vec"].n

        for c in range(NCH):
            t0 = c * 512
            off = c * 32
            for tt in range(4):
                @bk.tensor
                def _(t, c=c, t0=t0, tt=tt, off=off):
                    if c == 0 and tt == 0:
                        t.wait_ge(sm["vec"].h, ln_done)
                        t.wait_ge(sm["wl"].h, gw_wl)
                    if tt == 0:
                        t.wait_ge(sm["vec"].h, sm["vec"].n)
                        t.wait_ge(sm["sc"].h, sm["sc"].n)
                    for kt in range(DT):
                        sm["mm"].inc(t.matmul(
                            psG[:, (tt) * 8:(tt + 1) * 8],
                            XN[kt][:, t0 + tt * 128:t0 + (tt + 1) * 128],
                            GW[:, kt * 8:(kt + 1) * 8],
                            start=(kt == 0), stop=(kt == DT - 1)))

            @bk.vector
            def _(v, c=c, off=off):
                v.wait_ge(sm["mm"].h, sm["mm"].n)
                for tt in range(4):
                    v.reduce_max(RTw[:, off + tt:off + tt + 1], psG[:, tt * 8:(tt + 1) * 8], axis=AX.X)
                ctx.fl(v)
                for tt in range(4):
                    v.tensor_scalar_mul(RTw[:, off + 4 + tt:off + 5 + tt], RTw[:, off + tt:off + tt + 1], -1.0)
                v.tensor_copy(CQ[:, c * 32:(c + 1) * 32], psG[:, 0:32])
                sm["vec"].inc(ctx.fl(v))

            vv = sm["vec"].n

            @bk.sync
            def _(s, vv=vv, off=off):
                s.wait_ge(sm["vec"].h, vv)
                s.dma_start(out=rtd[:, off + 0:off + 8], in_=RTw[:, off + 0:off + 8]).then_inc(sm["ld"].h, 16)
            sm["ld"].n += 16
            l1 = sm["ld"].n

            @bk.sync
            def _(s, l1=l1, off=off):
                s.wait_ge(sm["ld"].h, l1)
                s.dma_start(out=RT[:, off + 0:off + 8], in_=rtd[:, off + 0:off + 8]).then_inc(sm["ld"].h, 16)
            sm["ld"].n += 16
            l2_ = sm["ld"].n

            if kind == "l2":
                @bk.vector
                def _(v, c=c, off=off, l2_=l2_):
                    v.wait_ge(sm["ld"].h, l2_)
                    for tt in range(4):
                        cs = CQ[:, c * 32 + tt * 8:c * 32 + (tt + 1) * 8]
                        v.tensor_single_scalar(scr[:, tt * 8:(tt + 1) * 8], cs,
                                               RT[:, off + tt:off + tt + 1], ALU.is_ge)
                    ctx.fl(v)
                    for tt in range(4):
                        cs = CQ[:, c * 32 + tt * 8:c * 32 + (tt + 1) * 8]
                        v.tensor_scalar(S2[:, tt * 8:(tt + 1) * 8],
                                        scr[:, tt * 8:(tt + 1) * 8], -1e30, None,
                                        ALU.mult)
                    ctx.fl(v)
                    for tt in range(4):
                        cs = CQ[:, c * 32 + tt * 8:c * 32 + (tt + 1) * 8]
                        v.tensor_add(S2[:, tt * 8:(tt + 1) * 8],
                                     S2[:, tt * 8:(tt + 1) * 8], cs)
                    ctx.fl(v)
                    for tt in range(4):
                        v.reduce_max(RTw[:, off + 8 + tt:off + 9 + tt],
                                     S2[:, tt * 8:(tt + 1) * 8], axis=AX.X)
                    ctx.fl(v)
                    for tt in range(4):
                        v.tensor_sub(RTw[:, off + 12 + tt:off + 13 + tt], RTw[:, off + 8 + tt:off + 9 + tt],
                                     RTw[:, off + tt:off + tt + 1])
                    sm["vec"].inc(ctx.fl(v))

                @bk.scalar
                def _(sc, off=off):
                    sc.wait_ge(sm["vec"].h, sm["vec"].n)
                    sm["sc"].inc(sc.activation(RTw[:, off + 20:off + 24], RTw[:, off + 12:off + 16], AF.Exp))

                @bk.vector
                def _(v, off=off):
                    v.wait_ge(sm["sc"].h, sm["sc"].n)
                    v.tensor_scalar_add(RTw[:, off + 16:off + 20], RTw[:, off + 20:off + 24], 1.0)
                    ctx.fl(v)
                    v.reciprocal(RTw[:, off + 24:off + 28], RTw[:, off + 16:off + 20])
                    ctx.fl(v)
                    v.tensor_mul(RTw[:, off + 28:off + 32], RTw[:, off + 20:off + 24],
                                 RTw[:, off + 24:off + 28])
                    sm["vec"].inc(ctx.fl(v))

                vv2 = sm["vec"].n

                @bk.sync
                def _(s, vv2=vv2, off=off):
                    s.wait_ge(sm["vec"].h, vv2)
                    s.dma_start(out=rtd[:, off + 8:off + 32], in_=RTw[:, off + 8:off + 32]).then_inc(sm["ld"].h, 16)
                sm["ld"].n += 16
                l3_ = sm["ld"].n

                @bk.sync
                def _(s, l3_=l3_, off=off):
                    s.wait_ge(sm["ld"].h, l3_)
                    s.dma_start(out=RT[:, off + 8:off + 32], in_=rtd[:, off + 8:off + 32]).then_inc(sm["ld"].h, 16)
                sm["ld"].n += 16
                l4_ = sm["ld"].n

                @bk.vector
                def _(v, c=c, off=off, l4_=l4_):
                    v.wait_ge(sm["ld"].h, l4_)
                    for tt in range(4):
                        cs = CQ[:, c * 32 + tt * 8:c * 32 + (tt + 1) * 8]
                        v.tensor_single_scalar(scr[:, 32 + tt * 8:32 + (tt + 1) * 8],
                                               S2[:, tt * 8:(tt + 1) * 8],
                                               RT[:, off + 8 + tt:off + 9 + tt], ALU.is_ge)
                        v.tensor_single_scalar(scr[:, tt * 8:(tt + 1) * 8], cs,
                                               RT[:, off + tt:off + tt + 1], ALU.is_ge)
                    ctx.fl(v)
                    for tt in range(4):
                        v.tensor_single_scalar(scr[:, 64 + tt * 8:64 + (tt + 1) * 8],
                                               scr[:, tt * 8:(tt + 1) * 8],
                                               RT[:, off + 24 + tt:off + 25 + tt], ALU.mult)
                        v.tensor_single_scalar(scr[:, 96 + tt * 8:96 + (tt + 1) * 8],
                                               scr[:, 32 + tt * 8:32 + (tt + 1) * 8],
                                               RT[:, off + 28 + tt:off + 29 + tt], ALU.mult)
                    ctx.fl(v)
                    for tt in range(4):
                        v.tensor_add(CQ[:, c * 32 + tt * 8:c * 32 + (tt + 1) * 8],
                                     scr[:, 64 + tt * 8:64 + (tt + 1) * 8],
                                     scr[:, 96 + tt * 8:96 + (tt + 1) * 8])
                    sm["vec"].inc(ctx.fl(v))
            else:
                @bk.scalar
                def _(sc, c=c, off=off, l2_=l2_):
                    sc.wait_ge(sm["ld"].h, l2_)
                    for tt in range(4):
                        ins = sc.activation(scr[:, tt * 8:(tt + 1) * 8],
                                            psG[:, tt * 8:(tt + 1) * 8], AF.Exp,
                                            bias=RT[:, off + 4 + tt:off + 5 + tt],
                                            accum_out=RTw[:, off + 8 + tt:off + 9 + tt])
                    sm["sc"].inc(ins)

                @bk.vector
                def _(v, off=off):
                    v.wait_ge(sm["sc"].h, sm["sc"].n)
                    v.tensor_scalar_mul(RTw[:, off + 16:off + 20], RTw[:, off + 8:off + 12], 1e-8)
                    ctx.fl(v)
                    v.tensor_scalar_add(RTw[:, off + 20:off + 24], RTw[:, off + 16:off + 20], 1.0)
                    ctx.fl(v)
                    v.reciprocal(RTw[:, off + 12:off + 16], RTw[:, off + 20:off + 24])
                    sm["vec"].inc(ctx.fl(v))

                vv2 = sm["vec"].n

                @bk.sync
                def _(s, vv2=vv2, off=off):
                    s.wait_ge(sm["vec"].h, vv2)
                    s.dma_start(out=rtd[:, off + 8:off + 16], in_=RTw[:, off + 8:off + 16]).then_inc(sm["ld"].h, 16)
                sm["ld"].n += 16
                l3_ = sm["ld"].n

                @bk.sync
                def _(s, l3_=l3_, off=off):
                    s.wait_ge(sm["ld"].h, l3_)
                    s.dma_start(out=RT[:, off + 8:off + 16], in_=rtd[:, off + 8:off + 16]).then_inc(sm["ld"].h, 16)
                sm["ld"].n += 16
                l4_ = sm["ld"].n

                @bk.vector
                def _(v, c=c, off=off, l4_=l4_):
                    v.wait_ge(sm["ld"].h, l4_)
                    for tt in range(4):
                        cs = CQ[:, c * 32 + tt * 8:c * 32 + (tt + 1) * 8]
                        v.tensor_single_scalar(scr[:, tt * 8:(tt + 1) * 8], cs,
                                               RT[:, off + tt:off + tt + 1], ALU.is_ge)
                    ctx.fl(v)
                    for tt in range(4):
                        cs = CQ[:, c * 32 + tt * 8:c * 32 + (tt + 1) * 8]
                        v.tensor_single_scalar(cs, scr[:, tt * 8:(tt + 1) * 8],
                                               RT[:, off + 12 + tt:off + 13 + tt], ALU.mult)
                    sm["vec"].inc(ctx.fl(v))

            @bk.tensor
            def _(t, c=c, off=off):
                t.wait_ge(sm["vec"].h, sm["vec"].n)
                for tt in range(4):
                    sm["mm"].inc(t.transpose(psT[0:8, tt * 128:(tt + 1) * 128],
                                             CQ[:, c * 32 + tt * 8:c * 32 + (tt + 1) * 8],
                                             ID128[:, :]))

            @bk.vector
            def _(v, c=c, off=off):
                v.wait_ge(sm["mm"].h, sm["mm"].n)
                sm["vec"].inc(v.tensor_copy(CROW[:, c * 512:(c + 1) * 512],
                                            psT[0:8, 0:512]))

        routing_done = sm["vec"].n

        seq = 0
        seq2 = 0
        wready2 = {}
        wfree2 = {}
        for e in range(NE):
            for c in range(NCH):
                t0 = c * 512

                @bk.tensor
                def _(t, e=e, c=c):
                    if e == 0 and c == 0:
                        t.wait_ge(sm["vec"].h, routing_done)
                    t.wait_ge(sm["vec"].h, sm["vec"].n)
                    for tt in range(4):
                        sm["mm"].inc(t.matmul(
                            psCb[:, tt * 128:(tt + 1) * 128],
                            SEL8[:, e * 128:(e + 1) * 128],
                            CROW[:, c * 512 + tt * 128:c * 512 + (tt + 1) * 128],
                            start=True, stop=True))

                for mt in range(MT):
                    b = seq % 2

                    @bk.gpsimd
                    def _(g, e=e, mt=mt, seq=seq, b=b):
                        if seq >= 2:
                            g.wait_ge(sm["mm"].h, wfree[seq - 2])
                        g.dma_start(
                            out=wbuf[:, b * D:b * D + D].rearrange(
                                "p (a q) -> p a q", a=DT),
                            in_=w1T[:, e * H + mt * 128:e * H + (mt + 1) * 128]
                            .rearrange("(a p) m -> p a m", p=128),
                        ).then_inc(sm["wl"].h, 16)
                        sm["wl"].n += 16
                        if kind == "l2":
                            g.dma_start(
                                out=wbuf2[:, b * H:b * H + D].rearrange(
                                    "p (a q) -> p a q", a=DT),
                                in_=w3T[:, e * H + mt * 128:e * H + (mt + 1) * 128]
                                .rearrange("(a p) m -> p a m", p=128),
                            ).then_inc(sm["wl"].h, 16)
                            sm["wl"].n += 16
                    wready[seq] = sm["wl"].n

                    @bk.tensor
                    def _(t, c=c, t0=t0, seq=seq, b=b):
                        t.wait_ge(sm["wl"].h, wready[seq])
                        t.wait_ge(sm["sc"].h, sm["sc"].n)
                        t.wait_ge(sm["vec"].h, sm["vec"].n)
                        for kt in range(DT):
                            sm["mm"].inc(t.matmul(
                                psH1[:, 0:512],
                                wbuf[:, b * D + kt * 128:b * D + (kt + 1) * 128],
                                XN[kt][:, t0:t0 + 512],
                                start=(kt == 0), stop=(kt == DT - 1)))
                        if kind == "l2":
                            for kt in range(DT):
                                sm["mm"].inc(t.matmul(
                                    psH3[:, 0:512],
                                    wbuf2[:, b * H + kt * 128:b * H + (kt + 1) * 128],
                                    XN[kt][:, t0:t0 + 512],
                                    start=(kt == 0), stop=(kt == DT - 1)))
                    wfree[seq] = sm["mm"].n
                    seq += 1

                    @bk.scalar
                    def _(sc, mt=mt):
                        sc.wait_ge(sm["mm"].h, sm["mm"].n)
                        sc.wait_ge(sm["vec"].h, sm["vec"].n)
                        sm["sc"].inc(sc.activation(
                            s1sb[:, 0:512], psH1[:, 0:512],
                            AF.Silu if kind == "l2" else AF.Gelu))

                    @bk.vector
                    def _(v, mt=mt):
                        v.wait_ge(sm["sc"].h, sm["sc"].n)
                        if kind == "l2":
                            v.tensor_mul(hgsb[:, 0:512], psH3[:, 0:512], s1sb[:, 0:512])
                            sm["vec"].inc(v.tensor_mul(
                                HG[:, mt * 512:(mt + 1) * 512], hgsb[:, 0:512],
                                psCb[:, 0:512]))
                        else:
                            sm["vec"].inc(v.tensor_mul(
                                HG[:, mt * 512:(mt + 1) * 512], s1sb[:, 0:512],
                                psCb[:, 0:512]))

                hg_done = sm["vec"].n

                for ot in range(DT):
                    b2 = seq2 % 2

                    @bk.gpsimd
                    def _(g, e=e, ot=ot, seq2=seq2, b2=b2):
                        if seq2 >= 2:
                            g.wait_ge(sm["mm"].h, wfree2[seq2 - 2])
                        else:
                            g.wait_ge(sm["mm"].h, wfree[seq - 1])
                        g.dma_start(
                            out=wbuf2[:, b2 * H:(b2 + 1) * H].rearrange(
                                "p (a q) -> p a q", a=MT),
                            in_=w2T[e * H:(e + 1) * H, ot * 128:(ot + 1) * 128]
                            .rearrange("(a p) m -> p a m", p=128),
                        ).then_inc(sm["wl"].h, 16)
                        sm["wl"].n += 16
                    wready2[seq2] = sm["wl"].n

                    @bk.tensor
                    def _(t, ot=ot, seq2=seq2, b2=b2, hg_done=hg_done):
                        t.wait_ge(sm["wl"].h, wready2[seq2])
                        t.wait_ge(sm["vec"].h, sm["vec"].n)
                        for kt in range(MT):
                            sm["mm"].inc(t.matmul(
                                psY[:, 0:512],
                                wbuf2[:, b2 * H + kt * 128:b2 * H + (kt + 1) * 128],
                                HG[:, kt * 512:(kt + 1) * 512],
                                start=(kt == 0), stop=(kt == MT - 1)))
                    wfree2[seq2] = sm["mm"].n
                    seq2 += 1

                    @bk.vector
                    def _(v, e=e, c=c, t0=t0, ot=ot):
                        v.wait_ge(sm["mm"].h, sm["mm"].n)
                        if e == 0:
                            sm["vec"].inc(v.tensor_copy(
                                ACC[ot][:, t0:t0 + 512], psY[:, 0:512]))
                        else:
                            sm["vec"].inc(v.tensor_add(
                                ACC[ot][:, t0:t0 + 512], ACC[ot][:, t0:t0 + 512],
                                psY[:, 0:512]))

        ob_st = {0: 0, 1: 0}
        for c in range(NCH):
            t0 = c * 512

            if kind == "l3":
                emit_ln_gb(ctx, bk, ACC, ACC, t0, 512, psA, psB, scr, srow, GL, BL)

            @bk.sync
            def _(s, c=c, t0=t0):
                s.wait_ge(sm["vec"].h, sm["vec"].n)
                s.wait_ge(sm["v2"].h, sm["v2"].n)
                dma_tiles(s, sm["ld"], Xb, DT, xd[:, t0:t0 + 512])
            xr_ld = sm["ld"].n

            for ot in range(DT):
                ob = OB[ot % 2]

                @bk.vector
                def _(v, c=c, t0=t0, ot=ot, ob=ob, xr_ld=xr_ld):
                    v.wait_ge(sm["ld"].h, xr_ld)
                    v.wait_ge(sm["st"].h, ob_st[ot % 2])
                    sm["v2"].inc(v.tensor_add(ob[:, 0:512],
                                              ACC[ot][:, t0:t0 + 512],
                                              XC[ot][:, 0:512]))
                v2n = sm["v2"].n

                @bk.sync
                def _(s, c=c, ot=ot, ob=ob, v2n=v2n):
                    s.wait_ge(sm["v2"].h, v2n)
                    s.dma_start(out=xo[ot * 128:(ot + 1) * 128, c * 512:(c + 1) * 512],
                                in_=ob[:, 0:512]).then_inc(sm["st"].h, 16)
                sm["st"].n += 16
                ob_st[ot % 2] = sm["st"].n

        if debug:
            dbg = nc.dram_tensor("dbg", [128, 4096], F32, kind="ExternalOutput").ap()

            @bk.gpsimd
            def _(g):
                g.wait_ge(sm["st"].h, sm["st"].n)
                g.dma_start(out=dbg[:, 0:128], in_=CQ[:, 0:128]).then_inc(sm["wl"].h, 16)
                g.dma_start(out=dbg[0:8, 128:2176], in_=CROW[:, 0:2048]).then_inc(sm["wl"].h, 16)
                g.dma_start(out=dbg[:, 2176:2304], in_=RT[:, 0:128]).then_inc(sm["wl"].h, 16)
                g.dma_start(out=dbg[:, 2304:2432], in_=RTw[:, 0:128]).then_inc(sm["wl"].h, 16)
                g.wait_ge(sm["wl"].h, sm["wl"].n + 64)
            sm["wl"].n += 64

        @bk.sync
        def _(s):
            s.wait_ge(sm["st"].h, sm["st"].n)

    ctx.close()
    return nc

def host_moe_inputs(kind, x, gate_w, w1, w2, w3=None, ln_g=None, ln_b=None):
    NE = 8
    H = 4096 if kind == "l2" else 2048
    gwT = np.ascontiguousarray(gate_w.T).astype(np.float32)
    if kind == "l2":
        w1Th = np.ascontiguousarray(
            np.concatenate([w1[e].T for e in range(NE)], axis=1)).astype(np.float32)
        w3Th = np.ascontiguousarray(
            np.concatenate([w3[e].T for e in range(NE)], axis=1)).astype(np.float32)
        w2Th = np.ascontiguousarray(
            np.concatenate([w2[e].T for e in range(NE)], axis=0)).astype(np.float32)
    else:
        w1Th = np.ascontiguousarray(
            np.concatenate([w1[e] for e in range(NE)], axis=1)).astype(np.float32)
        w2Th = np.ascontiguousarray(
            np.concatenate([w2[e] for e in range(NE)], axis=0)).astype(np.float32)
    sel8 = np.zeros((8, 8 * 128), np.float32)
    for e in range(8):
        sel8[e, e * 128:(e + 1) * 128] = 1.0
    id128 = np.eye(128, dtype=np.float32)
    maps = []
    for core in range(NCORES):
        b, q = divmod(core, 4)
        xT = np.ascontiguousarray(x[b, q * TC:(q + 1) * TC].T).astype(np.float32)
        m = {"xd": xT, "gwT": gwT, "w1T": w1Th, "w2T": w2Th,
             "sel8": sel8, "ident128": id128}
        if kind == "l2":
            m["w3T"] = w3Th
        else:
            m["gl"] = np.ascontiguousarray(ln_g.reshape(DT, 128).T).astype(np.float32)
            m["bl"] = np.ascontiguousarray(ln_b.reshape(DT, 128).T).astype(np.float32)
        maps.append(m)
    return maps


def run_moe(kind, x, gate_w, w1, w2, w3=None, ln_g=None, ln_b=None, nc=None):
    nc = nc or build_moe(kind)
    maps = host_moe_inputs(kind, x, gate_w, w1, w2, w3, ln_g, ln_b)
    res = run_bass_kernel_spmd(nc, maps, core_ids=list(range(NCORES)))
    out = np.empty((B, S, D), np.float32)
    for core in range(NCORES):
        b, q = divmod(core, 4)
        out[b, q * TC:(q + 1) * TC] = res.results[core]["xo"].T
    return out



def build_l3_attn():
    ctx = Ctx(psum_banks=6)
    nc = ctx.nc
    sm = ctx.sems
    sb = ctx.sb

    xd = nc.dram_tensor("xd", [D, TC], F32, kind="ExternalInput").ap()
    wqkvT = nc.dram_tensor("wqkvT", [D, 3 * D], F32, kind="ExternalInput").ap()
    woT = nc.dram_tensor("woT", [D, D], F32, kind="ExternalInput").ap()
    bqkvd = nc.dram_tensor("bqkv", [128, 3 * DT], F32, kind="ExternalInput").ap()
    bod = nc.dram_tensor("bo", [128, DT], F32, kind="ExternalInput").ap()
    identd = nc.dram_tensor("ident64d", [128, 64], BF16, kind="ExternalInput").ap()
    mrt = nc.dram_tensor("mrt", [128, 4], F32, kind="Internal").ap()
    xo = nc.dram_tensor("xo", [D, TC], F32, kind="ExternalOutput").ap()

    Xb, XC = ctx.tiles("xc", DT, 512, F32)
    XNb, XN = ctx.tiles("xn", DT, TC, BF16)
    Ab, ATT = ctx.tiles("at", DT, TC, BF16)
    OBb, OB = ctx.tiles("ob", 2, 512, F32)
    QKVT = sb("qkvt", [128, 24 * 512], BF16)
    KS = sb("ks", [64, 16 * 512], BF16)
    QS = sb("qs", [64, 512], BF16)
    VH = sb("vh", [128, 64 * VHS], BF16)
    BQKV = sb("bqkv_t", [128, 3 * DT], F32)
    BO = sb("bo_t", [128, DT], F32)
    ident64 = sb("ident64", [128, 64], BF16)
    vtmp = sb("vtmp", [128, 512], BF16)
    scr = sb("scr", [128, 1024], F32)
    srow = sb("srow", [1, 2048], F32)
    mcol = sb("mcol", [128, 4], F32)
    pbuf = sb("pbuf", [128, 512], BF16)
    linv = sb("linv", [1, 512], F32)
    linvb = sb("linvb", [64, 512], F32)
    wbuf = sb("wbuf", [128, 2 * D], BF16)

    psA, psB, psS, psO, psX, psM = ctx.ps
    pst = ctx.st.enter_context(nc.psum_tensor("pst", [128, 128], BF16)).ap()

    wready = {}
    wfree = {}
    seq = 0

    def load_w(wdram, mt, s):
        @bk.gpsimd
        def _(g, wdram=wdram, mt=mt, s=s):
            if s >= 2:
                g.wait_ge(sm["mm"].h, wfree[s - 2])
            g.dma_start(
                out=wbuf[:, (s % 2) * D:(s % 2) * D + D].rearrange(
                    "p (a q) -> p a q", a=DT),
                in_=wdram[:, mt * 128:(mt + 1) * 128].rearrange(
                    "(a p) m -> p a m", p=128),
            ).then_inc(sm["wl"].h, 16)
            sm["wl"].n += 16
        wready[s] = sm["wl"].n
        return wbuf[:, (s % 2) * D:(s % 2) * D + D]

    with nc.Block() as bk:
        @bk.vector
        def _(v):
            v.memset(ctx.ones_col[:, :], 1.0)
            v.memset(ctx.ones_row[:, :], 1.0)
            v.memset(ctx.flushbuf[:, :], 1.0)
            v.memset(VH[:, :], 1.0)
            sm["vec"].inc(v.memset(ctx.ones_row_bf[:, :], 1.0))

        @bk.sync
        def _(s):
            s.dma_start(out=BQKV[:, :], in_=bqkvd).then_inc(sm["ld"].h, 16)
            s.dma_start(out=BO[:, :], in_=bod).then_inc(sm["ld"].h, 16)
            s.dma_start(out=ident64[:, :], in_=identd).then_inc(sm["ld"].h, 16)
        sm["ld"].n += 48

        for c in range(TC // 512):
            t0 = c * 512

            @bk.sync
            def _(s, t0=t0):
                s.wait_ge(sm["vec"].h, sm["vec"].n)
                dma_tiles(s, sm["ld"], Xb, DT, xd[:, t0:t0 + 512])
            ld_now = sm["ld"].n

            @bk.tensor
            def _(t, ld_now=ld_now):
                t.wait_ge(sm["ld"].h, ld_now)

            @bk.scalar
            def _(sc, ld_now=ld_now):
                sc.wait_ge(sm["ld"].h, ld_now)

            emit_ln_chunk(ctx, bk, XC, XN, 0, t0, 512, psA, psB, scr, srow)
        ln_done = sm["vec"].n

        for u in range(4):
            u0 = u * 512

            for mt in range(24):
                wv = load_w(wqkvT, mt, seq)

                @bk.tensor
                def _(t, mt=mt, s=seq, u0=u0, wv=wv):
                    t.wait_ge(sm["wl"].h, wready[s])
                    if s == 0:
                        t.wait_ge(sm["vec"].h, ln_done)
                    t.wait_ge(sm["vec"].h, sm["vec"].n)
                    for kt in range(DT):
                        sm["mm"].inc(t.matmul(
                            psS[:, 0:512], wv[:, kt * 128:(kt + 1) * 128],
                            XN[kt][:, u0:u0 + 512],
                            start=(kt == 0), stop=(kt == DT - 1)))
                wfree[seq] = sm["mm"].n
                seq += 1

                @bk.vector
                def _(v, mt=mt):
                    v.wait_ge(sm["mm"].h, sm["mm"].n)
                    sm["vec"].inc(v.tensor_scalar_add(
                        QKVT[:, mt * 512:(mt + 1) * 512], psS[:, 0:512],
                        BQKV[:, mt:mt + 1]))
            qkv_vec = sm["vec"].n

            for jp in range(16):
                vrow = 192 * jp + 128
                vt, vr = vrow // 128, vrow % 128
                for rs in range(4):
                    @bk.tensor
                    def _(t, jp=jp, rs=rs, vt=vt, vr=vr, qkv_vec=qkv_vec):
                        t.wait_ge(sm["vec"].h, sm["vec"].n)
                        sm["mm"].inc(t.transpose(
                            pst[:, 0:64],
                            QKVT[vr:vr + 64,
                                 vt * 512 + rs * 128:vt * 512 + (rs + 1) * 128],
                            ident64[vr:vr + 64, :]))

                    @bk.vector
                    def _(v, jp=jp, rs=rs):
                        v.wait_ge(sm["mm"].h, sm["mm"].n)
                        idx = (jp * 4 + rs) * VHS
                        sm["vec"].inc(v.tensor_copy(VH[:, idx:idx + 64], pst[:, 0:64]))

            for jp in range(16):
                krow = 192 * jp + 64
                kt_, kr = krow // 128, krow % 128

                @bk.vector
                def _(v, jp=jp, kt_=kt_, kr=kr, qkv_vec=qkv_vec):
                    if jp == 0:
                        v.wait_ge(sm["vec"].h, qkv_vec)
                    sm["vec"].inc(v.tensor_copy(
                        KS[0:64, jp * 512:(jp + 1) * 512],
                        QKVT[kr:kr + 64, kt_ * 512:(kt_ + 1) * 512]))

            for phase, roff, dst in (("q", 0, 0), ("k", 64, 1)):
                for jp in range(16):
                    row = 192 * jp + roff
                    rt_, rr = row // 128, row % 128

                    @bk.scalar
                    def _(sc, jp=jp, rt_=rt_, rr=rr):
                        sc.wait_ge(sm["vec"].h, sm["vec"].n)
                        sc.wait_ge(sm["mm"].h, sm["mm"].n)
                        sm["sc"].inc(sc.activation(
                            scr[rr:rr + 64, 0:512],
                            QKVT[rr:rr + 64, rt_ * 512:(rt_ + 1) * 512], AF.Square))

                    @bk.tensor
                    def _(t, rr=rr):
                        t.wait_ge(sm["sc"].h, sm["sc"].n)
                        t.wait_ge(sm["vec"].h, sm["vec"].n)
                        sm["mm"].inc(t.matmul(
                            psM[0 if rr == 0 else 64:(0 if rr == 0 else 64) + 1, 0:512],
                            ctx.ones_col[rr:rr + 64, :], scr[rr:rr + 64, 0:512],
                            start=True, stop=True))

                    @bk.vector
                    def _(v, phase=phase, jp=jp, dst=dst, rr=rr):
                        v.wait_ge(sm["mm"].h, sm["mm"].n)
                        sm["vec"].inc(v.reduce_max(
                            srow[0:1, dst * 16 + jp:dst * 16 + jp + 1],
                            psM[0 if rr == 0 else 64:(0 if rr == 0 else 64) + 1, 0:512],
                            axis=AX.X))

            @bk.vector
            def _(v):
                ctx.fl(v)
                v.reduce_max(srow[0:1, 32:33], srow[0:1, 0:16], axis=AX.X)
                v.reduce_max(srow[0:1, 33:34], srow[0:1, 16:32], axis=AX.X)
                ctx.fl(v)
                v.tensor_mul(srow[0:1, 34:35], srow[0:1, 32:33], srow[0:1, 33:34])
                sm["vec"].inc(ctx.fl(v))

            @bk.scalar
            def _(sc):
                sc.wait_ge(sm["vec"].h, sm["vec"].n)
                sc.activation(srow[0:1, 34:35], srow[0:1, 34:35], AF.Ln)
                sm["sc"].inc(sc.activation(srow[0:1, 35:36], srow[0:1, 34:35],
                                           AF.Exp, scale=0.5))

            @bk.vector
            def _(v):
                v.wait_ge(sm["sc"].h, sm["sc"].n)
                v.tensor_single_scalar(srow[0:1, 128:256], ctx.ones_row[0:1, :],
                                       srow[0:1, 35:36], ALU.mult)
                sm["vec"].inc(ctx.fl(v))

            @bk.tensor
            def _(t):
                t.wait_ge(sm["vec"].h, sm["vec"].n)
                sm["mm"].inc(t.matmul(psM[0:128, 0:128], ctx.ones_row[:, :],
                                      srow[0:1, 128:256], start=True, stop=True))

            @bk.vector
            def _(v, u=u):
                v.wait_ge(sm["mm"].h, sm["mm"].n)
                v.tensor_scalar_mul(mcol[:, u:u + 1], psM[0:128, 0:1], -0.125)
                sm["vec"].inc(ctx.fl(v))
            mv = sm["vec"].n

            @bk.sync
            def _(s, u=u, mv=mv):
                s.wait_ge(sm["vec"].h, mv)
                s.dma_start(out=mrt[:, 0:4], in_=mcol[:, 0:4]
                            ).then_inc(sm["ld"].h, 16)
            sm["ld"].n += 16
            mo = sm["ld"].n

            @bk.sync
            def _(s, u=u, mo=mo):
                s.wait_ge(sm["ld"].h, mo)
                s.dma_start(out=mcol[:, 0:4], in_=mrt[:, 0:4]
                            ).then_inc(sm["ld"].h, 16)
            sm["ld"].n += 16
            m_ld = sm["ld"].n

            for qj in range(16):
                qrow = 192 * qj
                qt_, qr = qrow // 128, qrow % 128

                @bk.vector
                def _(v, qt_=qt_, qr=qr):
                    v.wait_ge(sm["mm"].h, sm["mm"].n)
                    sm["vec"].inc(v.tensor_copy(
                        QS[0:64, 0:512], QKVT[qr:qr + 64, qt_ * 512:(qt_ + 1) * 512]))
                qs_vec = sm["vec"].n

                for kt in range(64):
                    @bk.tensor
                    def _(t, kt=kt, m_ld=m_ld, qs_vec=qs_vec):
                        if kt == 0:
                            t.wait_ge(sm["ld"].h, m_ld)
                            t.wait_ge(sm["vec"].h, qs_vec)
                        t.wait_ge(sm["sc"].h, sm["sc"].n)
                        sm["mm"].inc(t.matmul(
                            psS[:, 0:512],
                            KS[0:64, kt * 128:(kt + 1) * 128],
                            QS[0:64, 0:512],
                            start=True, stop=True))

                    @bk.scalar
                    def _(sc, u=u, kt=kt):
                        sc.wait_ge(sm["mm"].h, sm["mm"].n)
                        sc.wait_ge(sm["vec"].h, sm["vec"].n)
                        sm["sc"].inc(sc.activation(pbuf[:, 0:512], psS[:, 0:512],
                                                   AF.Exp, scale=0.125,
                                                   bias=mcol[:, u:u + 1]))

                    @bk.tensor
                    def _(t, kt=kt):
                        t.wait_ge(sm["sc"].h, sm["sc"].n)
                        idx = kt * VHS
                        sm["mm"].inc(t.matmul(psO[0:65, 0:512], VH[:, idx:idx + 65],
                                              pbuf[:, 0:512],
                                              start=(kt == 0), stop=(kt == 63)))

                @bk.vector
                def _(v):
                    v.wait_ge(sm["mm"].h, sm["mm"].n)
                    v.tensor_scalar_add(linv[0:1, 0:512], psO[64:65, 0:512], 1e-30)
                    sm["vec"].inc(v.reciprocal(linv[0:1, 0:512], linv[0:1, 0:512]))

                @bk.tensor
                def _(t):
                    t.wait_ge(sm["vec"].h, sm["vec"].n)
                    sm["mm"].inc(t.matmul(psM[0:64, 0:512], ctx.ones_row[0:1, 0:64],
                                          linv[0:1, 0:512], start=True, stop=True))

                @bk.vector
                def _(v, u=u, qj=qj):
                    v.wait_ge(sm["mm"].h, sm["mm"].n)
                    v.tensor_copy(linvb[:, 0:512], psM[0:64, 0:512])
                    at, ar = qj // 2, 64 * (qj % 2)
                    sm["vec"].inc(v.tensor_mul(
                        ATT[at][ar:ar + 64, u * 512:(u + 1) * 512],
                        psO[0:64, 0:512], linvb[:, 0:512]))

        att_done = sm["vec"].n

        ob_st = {0: 0, 1: 0}
        for c in range(TC // 512):
            t0 = c * 512

            @bk.sync
            def _(s, c=c, t0=t0):
                s.wait_ge(sm["vec"].h, ln_done)
                s.wait_ge(sm["v2"].h, sm["v2"].n)
                dma_tiles(s, sm["ld"], Xb, DT, xd[:, t0:t0 + 512])
            xr_ld = sm["ld"].n

            for mt in range(DT):
                wv = load_w(woT, mt, seq)

                @bk.tensor
                def _(t, s=seq, t0=t0, wv=wv, att_done=att_done):
                    t.wait_ge(sm["wl"].h, wready[s])
                    t.wait_ge(sm["vec"].h, att_done)
                    t.wait_ge(sm["v2"].h, sm["v2"].n)
                    for kt in range(DT):
                        sm["mm"].inc(t.matmul(psX[:, 0:512],
                                              wv[:, kt * 128:(kt + 1) * 128],
                                              ATT[kt][:, t0:t0 + 512],
                                              start=(kt == 0), stop=(kt == DT - 1)))
                wfree[seq] = sm["mm"].n
                seq += 1

                ob = OB[mt % 2]

                @bk.vector
                def _(v, mt=mt, ob=ob, xr_ld=xr_ld):
                    v.wait_ge(sm["mm"].h, sm["mm"].n)
                    v.wait_ge(sm["ld"].h, xr_ld)
                    v.wait_ge(sm["st"].h, ob_st[mt % 2])
                    sm["v2"].inc(v.scalar_tensor_tensor(
                        ob[:, 0:512], psX[:, 0:512], BO[:, mt:mt + 1],
                        XC[mt][:, 0:512], ALU.add, ALU.add))
                v2n = sm["v2"].n

                @bk.sync
                def _(s, mt=mt, c=c, ob=ob, v2n=v2n):
                    s.wait_ge(sm["v2"].h, v2n)
                    s.dma_start(out=xo[mt * 128:(mt + 1) * 128, c * 512:(c + 1) * 512],
                                in_=ob[:, 0:512]).then_inc(sm["st"].h, 16)
                sm["st"].n += 16
                ob_st[mt % 2] = sm["st"].n

        @bk.sync
        def _(s):
            s.wait_ge(sm["st"].h, sm["st"].n)

    ctx.close()
    return nc


def host_l3_attn_inputs(x, Wqkv, bqkv, Wo, bo):
    ident = np.concatenate([np.eye(64), np.eye(64)], axis=0).astype(BF)
    wqkvT = np.ascontiguousarray(Wqkv.T).astype(np.float32)
    woT = np.ascontiguousarray(Wo.T).astype(np.float32)
    bq = np.ascontiguousarray(bqkv.reshape(3 * DT, 128).T).astype(np.float32)
    boh = np.ascontiguousarray(bo.reshape(DT, 128).T).astype(np.float32)
    maps = []
    for core in range(NCORES):
        b, q = divmod(core, 4)
        xT = np.ascontiguousarray(x[b, q * TC:(q + 1) * TC].T).astype(np.float32)
        maps.append({"xd": xT, "wqkvT": wqkvT, "woT": woT, "bqkv": bq, "bo": boh,
                     "ident64d": ident})
    return maps


def run_l3_attn(x, Wqkv, bqkv, Wo, bo, nc=None):
    nc = nc or build_l3_attn()
    maps = host_l3_attn_inputs(x, Wqkv, bqkv, Wo, bo)
    res = run_bass_kernel_spmd(nc, maps, core_ids=list(range(NCORES)))
    out = np.empty((B, S, D), np.float32)
    for core in range(NCORES):
        b, q = divmod(core, 4)
        out[b, q * TC:(q + 1) * TC] = res.results[core]["xo"].T
    return out



SFULL = 8192
NCHF = SFULL // 512


def build_l1_attn():
    ctx = Ctx(psum_banks=6)
    nc = ctx.nc
    sm = ctx.sems
    sb = ctx.sb

    xf = nc.dram_tensor("xf", [D, SFULL], F32, kind="ExternalInput").ap()
    wqkvT = nc.dram_tensor("wqkvT", [D, 3 * D], F32, kind="ExternalInput").ap()
    woT = nc.dram_tensor("woT", [D, D], F32, kind="ExternalInput").ap()
    bod = nc.dram_tensor("bo", [128, DT], F32, kind="ExternalInput").ap()
    identd = nc.dram_tensor("ident64d", [128, 64], BF16, kind="ExternalInput").ap()
    kdram = nc.dram_tensor("kdram", [D, SFULL], BF16, kind="Internal").ap()
    vhdram = nc.dram_tensor("vhdram", [128, NCHF * 16 * 4 * VHS], BF16,
                            kind="Internal").ap()
    mrt = nc.dram_tensor("mrt", [128, 16], F32, kind="Internal").ap()
    xo = nc.dram_tensor("xo", [D, TC], F32, kind="ExternalOutput").ap()

    Xb, XC = ctx.tiles("xc", DT, 512, F32)
    XNb, XN = ctx.tiles("xn", DT, 512, BF16)
    Qb, QT = ctx.tiles("qt", DT, TC, BF16)
    Ab, ATT = ctx.tiles("at", DT, TC, BF16)
    OBb, OB = ctx.tiles("ob", 2, 512, F32)
    KS = sb("ks", [64, SFULL], BF16)
    QS = sb("qs", [64, 512], BF16)
    VH = sb("vh", [128, 16 * 4 * VHS], BF16)
    SCK = sb("sck", [64, SFULL], F32)
    BO = sb("bo_t", [128, DT], F32)
    ident64 = sb("ident64", [128, 64], BF16)
    vtmp = sb("vtmp", [128, 512], BF16)
    scr = sb("scr", [128, 1024], F32)
    srow = sb("srow", [1, 2048], F32)
    mcol = sb("mcol", [128, 16], F32)
    pbuf = sb("pbuf", [128, 512], BF16)
    linv = sb("linv", [1, 512], F32)
    linvb = sb("linvb", [64, 512], F32)
    wbuf = sb("wbuf", [128, 2 * D], BF16)

    psA, psB, psS, psO, psX, psM = ctx.ps
    pst = ctx.st.enter_context(nc.psum_tensor("pst", [128, 128], BF16)).ap()

    wready = {}
    wfree = {}
    seq = 0

    def load_w(wdram, col0, mt, s):
        @bk.gpsimd
        def _(g, wdram=wdram, col0=col0, mt=mt, s=s):
            if s >= 2:
                g.wait_ge(sm["mm"].h, wfree[s - 2])
            g.dma_start(
                out=wbuf[:, (s % 2) * D:(s % 2) * D + D].rearrange(
                    "p (a q) -> p a q", a=DT),
                in_=wdram[:, col0 + mt * 128:col0 + (mt + 1) * 128].rearrange(
                    "(a p) m -> p a m", p=128),
            ).then_inc(sm["wl"].h, 16)
            sm["wl"].n += 16
        wready[s] = sm["wl"].n
        return wbuf[:, (s % 2) * D:(s % 2) * D + D]

    with nc.Block() as bk:
        @bk.vector
        def _(v):
            v.memset(ctx.ones_col[:, :], 1.0)
            v.memset(ctx.ones_row[:, :], 1.0)
            v.memset(ctx.flushbuf[:, :], 1.0)
            v.memset(VH[:, :], 1.0)
            v.memset(mcol[:, :], 0.0)
            sm["vec"].inc(v.memset(ctx.ones_row_bf[:, :], 1.0))

        @bk.sync
        def _(s):
            s.dma_start(out=BO[:, :], in_=bod).then_inc(sm["ld"].h, 16)
            s.dma_start(out=ident64[:, :], in_=identd).then_inc(sm["ld"].h, 16)
        sm["ld"].n += 32

        for c in range(NCHF):
            t0 = c * 512

            @bk.sync
            def _(s, t0=t0):
                s.wait_ge(sm["vec"].h, sm["vec"].n)
                s.wait_ge(sm["wl"].h, sm["wl"].n)
                dma_tiles(s, sm["ld"], Xb, DT, xf[:, t0:t0 + 512])
            ld_now = sm["ld"].n

            @bk.tensor
            def _(t, ld_now=ld_now):
                t.wait_ge(sm["ld"].h, ld_now)

            @bk.scalar
            def _(sc, ld_now=ld_now):
                sc.wait_ge(sm["ld"].h, ld_now)

            emit_ln_chunk(ctx, bk, XC, XN, 0, 0, 512, psA, psB, scr, srow)

            nmt = 24 if c < 4 else 16
            for mi in range(nmt):
                if mi < 8:
                    col0, mt = D, mi
                elif mi < 16:
                    col0, mt = 2 * D, mi - 8
                else:
                    col0, mt = 0, mi - 16
                wv = load_w(wqkvT, col0, mt, seq)

                @bk.tensor
                def _(t, s=seq, wv=wv):
                    t.wait_ge(sm["wl"].h, wready[s])
                    t.wait_ge(sm["vec"].h, sm["vec"].n)
                    for kt in range(DT):
                        sm["mm"].inc(t.matmul(
                            psS[:, 0:512], wv[:, kt * 128:(kt + 1) * 128],
                            XN[kt][:, 0:512],
                            start=(kt == 0), stop=(kt == DT - 1)))
                wfree[seq] = sm["mm"].n
                seq += 1

                if mi < 8:
                    @bk.vector
                    def _(v, c=c, mt=mt):
                        v.wait_ge(sm["mm"].h, sm["mm"].n)
                        v.wait_ge(sm["wl"].h, sm["wl"].n)
                        sm["vec"].inc(v.tensor_copy(vtmp[:, 0:512], psS[:, 0:512]))
                    vkn = sm["vec"].n

                    @bk.gpsimd
                    def _(g, c=c, mt=mt, vkn=vkn):
                        g.wait_ge(sm["vec"].h, vkn)
                        g.dma_start(out=kdram[mt * 128:(mt + 1) * 128,
                                              c * 512:(c + 1) * 512],
                                    in_=vtmp[:, 0:512]).then_inc(sm["wl"].h, 16)
                        sm["wl"].n += 16
                elif mi < 16:
                    @bk.vector
                    def _(v):
                        v.wait_ge(sm["mm"].h, sm["mm"].n)
                        v.wait_ge(sm["mm"].h, sm["mm"].n)
                        sm["vec"].inc(v.tensor_copy(vtmp[:, 0:512], psS[:, 0:512]))

                    for sub in range(2):
                        for jt in range(4):
                            @bk.tensor
                            def _(t, sub=sub, jt=jt):
                                t.wait_ge(sm["vec"].h, sm["vec"].n)
                                sm["mm"].inc(t.transpose(
                                    pst[:, 0:64],
                                    vtmp[64 * sub:64 * sub + 64,
                                         jt * 128:(jt + 1) * 128],
                                    ident64[64 * sub:64 * sub + 64, :]))

                            @bk.vector
                            def _(v, mt=mt, sub=sub, jt=jt):
                                v.wait_ge(sm["mm"].h, sm["mm"].n)
                                h = 2 * mt + sub
                                idx = (h * 4 + jt) * VHS
                                sm["vec"].inc(v.tensor_copy(VH[:, idx:idx + 64],
                                                            pst[:, 0:64]))
                else:
                    @bk.vector
                    def _(v, c=c, mt=mt):
                        v.wait_ge(sm["mm"].h, sm["mm"].n)
                        sm["vec"].inc(v.tensor_copy(
                            QT[mt][:, c * 512:(c + 1) * 512], psS[:, 0:512]))

            vdone = sm["vec"].n

            @bk.gpsimd
            def _(g, c=c, vdone=vdone):
                g.wait_ge(sm["vec"].h, vdone)
                g.dma_start(out=vhdram[:, c * 16 * 4 * VHS:(c + 1) * 16 * 4 * VHS],
                            in_=VH[:, :]).then_inc(sm["wl"].h, 16)
                sm["wl"].n += 16

        for h in range(16):
            ht, hr = h // 2, 64 * (h % 2)

            @bk.sync
            def _(s, h=h):
                s.wait_ge(sm["wl"].h, sm["wl"].n)
                s.wait_ge(sm["mm"].h, sm["mm"].n)
                s.dma_start(out=KS[:, :], in_=kdram[h * 64:(h + 1) * 64, :]
                            ).then_inc(sm["ld"].h, 16)
                for c in range(NCHF):
                    s.dma_start(out=VH[:, c * 4 * VHS:(c + 1) * 4 * VHS],
                                in_=vhdram[:, (c * 16 + h) * 4 * VHS:
                                           (c * 16 + h) * 4 * VHS + 4 * VHS]
                                ).then_inc(sm["ld"].h, 16)
            sm["ld"].n += 16 * 17
            k_ld = sm["ld"].n

            @bk.scalar
            def _(sc, k_ld=k_ld):
                sc.wait_ge(sm["ld"].h, k_ld)
                sc.wait_ge(sm["vec"].h, sm["vec"].n)
                sm["sc"].inc(sc.activation(SCK[0:64, :], KS[0:64, :], AF.Square))

            for cc in range(NCHF):
                @bk.tensor
                def _(t, cc=cc):
                    t.wait_ge(sm["sc"].h, sm["sc"].n)
                    t.wait_ge(sm["vec"].h, sm["vec"].n)
                    sm["mm"].inc(t.matmul(psM[0:1, 0:512], ctx.ones_col[0:64, :],
                                          SCK[0:64, cc * 512:(cc + 1) * 512],
                                          start=True, stop=True))

                @bk.vector
                def _(v, cc=cc):
                    v.wait_ge(sm["mm"].h, sm["mm"].n)
                    sm["vec"].inc(v.reduce_max(srow[0:1, cc:cc + 1],
                                               psM[0:1, 0:512], axis=AX.X))

            @bk.scalar
            def _(sc, ht=ht, hr=hr):
                sc.wait_ge(sm["vec"].h, sm["vec"].n)
                sm["sc"].inc(sc.activation(SCK[0:64, 0:TC],
                                           QT[ht][hr:hr + 64, 0:TC], AF.Square))

            for cc in range(4):
                @bk.tensor
                def _(t, cc=cc):
                    t.wait_ge(sm["sc"].h, sm["sc"].n)
                    t.wait_ge(sm["vec"].h, sm["vec"].n)
                    sm["mm"].inc(t.matmul(psM[0:1, 0:512], ctx.ones_col[0:64, :],
                                          SCK[0:64, cc * 512:(cc + 1) * 512],
                                          start=True, stop=True))

                @bk.vector
                def _(v, cc=cc):
                    v.wait_ge(sm["mm"].h, sm["mm"].n)
                    sm["vec"].inc(v.reduce_max(srow[0:1, 16 + cc:17 + cc],
                                               psM[0:1, 0:512], axis=AX.X))

            @bk.vector
            def _(v):
                ctx.fl(v)
                v.reduce_max(srow[0:1, 32:33], srow[0:1, 0:16], axis=AX.X)
                v.reduce_max(srow[0:1, 33:34], srow[0:1, 16:20], axis=AX.X)
                ctx.fl(v)
                v.tensor_mul(srow[0:1, 34:35], srow[0:1, 32:33], srow[0:1, 33:34])
                sm["vec"].inc(ctx.fl(v))

            @bk.scalar
            def _(sc):
                sc.wait_ge(sm["vec"].h, sm["vec"].n)
                sc.activation(srow[0:1, 34:35], srow[0:1, 34:35], AF.Ln)
                sm["sc"].inc(sc.activation(srow[0:1, 35:36], srow[0:1, 34:35],
                                           AF.Exp, scale=0.5))

            @bk.vector
            def _(v):
                v.wait_ge(sm["sc"].h, sm["sc"].n)
                v.tensor_single_scalar(srow[0:1, 128:256], ctx.ones_row[0:1, :],
                                       srow[0:1, 35:36], ALU.mult)
                sm["vec"].inc(ctx.fl(v))

            @bk.tensor
            def _(t):
                t.wait_ge(sm["vec"].h, sm["vec"].n)
                sm["mm"].inc(t.matmul(psM[0:128, 0:128], ctx.ones_row[:, :],
                                      srow[0:1, 128:256], start=True, stop=True))

            @bk.vector
            def _(v, h=h):
                v.wait_ge(sm["mm"].h, sm["mm"].n)
                v.tensor_scalar_mul(mcol[:, h:h + 1], psM[0:128, 0:1], -0.125)
                sm["vec"].inc(ctx.fl(v))
            mv = sm["vec"].n

            @bk.sync
            def _(s, mv=mv):
                s.wait_ge(sm["vec"].h, mv)
                s.dma_start(out=mrt[:, 0:16], in_=mcol[:, 0:16]).then_inc(sm["ld"].h, 16)
            sm["ld"].n += 16
            mo = sm["ld"].n

            @bk.sync
            def _(s, mo=mo):
                s.wait_ge(sm["ld"].h, mo)
                s.dma_start(out=mcol[:, 0:16], in_=mrt[:, 0:16]).then_inc(sm["ld"].h, 16)
            sm["ld"].n += 16
            m_ld = sm["ld"].n

            for c in range(4):
                @bk.vector
                def _(v, ht=ht, hr=hr, c=c):
                    v.wait_ge(sm["mm"].h, sm["mm"].n)
                    sm["vec"].inc(v.tensor_copy(QS[0:64, 0:512],
                                                QT[ht][hr:hr + 64,
                                                       c * 512:(c + 1) * 512]))
                qs_vec = sm["vec"].n

                for kt in range(64):
                    @bk.tensor
                    def _(t, kt=kt, m_ld=m_ld, qs_vec=qs_vec):
                        if kt == 0:
                            t.wait_ge(sm["ld"].h, m_ld)
                            t.wait_ge(sm["vec"].h, qs_vec)
                        t.wait_ge(sm["sc"].h, sm["sc"].n)
                        sm["mm"].inc(t.matmul(psS[:, 0:512],
                                              KS[0:64, kt * 128:(kt + 1) * 128],
                                              QS[0:64, 0:512],
                                              start=True, stop=True))

                    @bk.scalar
                    def _(sc, h=h):
                        sc.wait_ge(sm["mm"].h, sm["mm"].n)
                        sm["sc"].inc(sc.activation(pbuf[:, 0:512], psS[:, 0:512],
                                                   AF.Exp, scale=0.125,
                                                   bias=mcol[:, h:h + 1]))

                    @bk.tensor
                    def _(t, kt=kt):
                        t.wait_ge(sm["sc"].h, sm["sc"].n)
                        jc, jt = kt // 4, kt % 4
                        idx = (jc * 4 + jt) * VHS
                        sm["mm"].inc(t.matmul(psO[0:65, 0:512], VH[:, idx:idx + 65],
                                              pbuf[:, 0:512],
                                              start=(kt == 0), stop=(kt == 63)))

                @bk.vector
                def _(v):
                    v.wait_ge(sm["mm"].h, sm["mm"].n)
                    v.tensor_scalar_add(linv[0:1, 0:512], psO[64:65, 0:512], 1e-30)
                    sm["vec"].inc(v.reciprocal(linv[0:1, 0:512], linv[0:1, 0:512]))

                @bk.tensor
                def _(t):
                    t.wait_ge(sm["vec"].h, sm["vec"].n)
                    sm["mm"].inc(t.matmul(psM[0:64, 0:512], ctx.ones_row[0:1, 0:64],
                                          linv[0:1, 0:512], start=True, stop=True))

                @bk.vector
                def _(v, ht=ht, hr=hr, c=c):
                    v.wait_ge(sm["mm"].h, sm["mm"].n)
                    v.tensor_copy(linvb[:, 0:512], psM[0:64, 0:512])
                    sm["vec"].inc(v.tensor_mul(
                        ATT[ht][hr:hr + 64, c * 512:(c + 1) * 512],
                        psO[0:64, 0:512], linvb[:, 0:512]))

        att_done = sm["vec"].n

        ob_st = {0: 0, 1: 0}
        for c in range(4):
            t0 = c * 512

            @bk.sync
            def _(s, t0=t0):
                s.wait_ge(sm["vec"].h, sm["vec"].n)
                s.wait_ge(sm["v2"].h, sm["v2"].n)
                dma_tiles(s, sm["ld"], Xb, DT, xf[:, t0:t0 + 512])
            xr_ld = sm["ld"].n

            for mt in range(DT):
                wv = load_w(woT, 0, mt, seq)

                @bk.tensor
                def _(t, s=seq, t0=t0, wv=wv, att_done=att_done):
                    t.wait_ge(sm["wl"].h, wready[s])
                    t.wait_ge(sm["vec"].h, att_done)
                    t.wait_ge(sm["v2"].h, sm["v2"].n)
                    for kt in range(DT):
                        sm["mm"].inc(t.matmul(psX[:, 0:512],
                                              wv[:, kt * 128:(kt + 1) * 128],
                                              ATT[kt][:, t0:t0 + 512],
                                              start=(kt == 0), stop=(kt == DT - 1)))
                wfree[seq] = sm["mm"].n
                seq += 1

                ob = OB[mt % 2]

                @bk.vector
                def _(v, mt=mt, ob=ob, xr_ld=xr_ld):
                    v.wait_ge(sm["mm"].h, sm["mm"].n)
                    v.wait_ge(sm["ld"].h, xr_ld)
                    v.wait_ge(sm["st"].h, ob_st[mt % 2])
                    sm["v2"].inc(v.scalar_tensor_tensor(
                        ob[:, 0:512], psX[:, 0:512], BO[:, mt:mt + 1],
                        XC[mt][:, 0:512], ALU.add, ALU.add))
                v2n = sm["v2"].n

                @bk.sync
                def _(s, mt=mt, c=c, ob=ob, v2n=v2n):
                    s.wait_ge(sm["v2"].h, v2n)
                    s.dma_start(out=xo[mt * 128:(mt + 1) * 128, c * 512:(c + 1) * 512],
                                in_=ob[:, 0:512]).then_inc(sm["st"].h, 16)
                sm["st"].n += 16
                ob_st[mt % 2] = sm["st"].n

        @bk.sync
        def _(s):
            s.wait_ge(sm["st"].h, sm["st"].n)

    ctx.close()
    return nc


def host_l1_attn_inputs(x, Wq, Wk, Wv, Wo, bo):
    ident = np.concatenate([np.eye(64), np.eye(64)], axis=0).astype(BF)
    wqkvT = np.ascontiguousarray(
        np.concatenate([Wq.T, Wk.T, Wv.T], axis=1)).astype(np.float32)
    woT = np.ascontiguousarray(Wo.T).astype(np.float32)
    boh = np.ascontiguousarray(bo.reshape(DT, 128).T).astype(np.float32)
    maps = []
    for core in range(NCORES):
        b, q = divmod(core, 4)
        t0 = q * TC
        xp = np.concatenate([x[b, t0:t0 + TC], x[b, :t0], x[b, t0 + TC:]], axis=0)
        xT = np.ascontiguousarray(xp.T).astype(np.float32)
        maps.append({"xf": xT, "wqkvT": wqkvT, "woT": woT, "bo": boh,
                     "ident64d": ident})
    return maps


def run_l1_attn(x, Wq, Wk, Wv, Wo, bo, nc=None):
    nc = nc or build_l1_attn()
    maps = host_l1_attn_inputs(x, Wq, Wk, Wv, Wo, bo)
    res = run_bass_kernel_spmd(nc, maps, core_ids=list(range(NCORES)))
    out = np.empty((B, S, D), np.float32)
    for core in range(NCORES):
        b, q = divmod(core, 4)
        out[b, q * TC:(q + 1) * TC] = res.results[core]["xo"].T
    return out



DI = 2048
DIT = DI // 128
H2A = 512
TH2 = TC + H2A


def build_l2a():
    ctx = Ctx(psum_banks=6)
    nc = ctx.nc
    sm = ctx.sems
    sb = ctx.sb

    xh = nc.dram_tensor("xh", [D, TH2], F32, kind="ExternalInput").ap()
    winT = nc.dram_tensor("winT", [D, 2 * DI], F32, kind="ExternalInput").ap()
    convw = nc.dram_tensor("convw", [128, 4 * DIT], F32, kind="ExternalInput").ap()
    convb = nc.dram_tensor("convb", [128, DIT], F32, kind="ExternalInput").ap()
    wxa = nc.dram_tensor("wxa", [DI, 64], F32, kind="ExternalInput").ap()
    wxb = nc.dram_tensor("wxb", [DI, 16], F32, kind="ExternalInput").ap()
    wxc = nc.dram_tensor("wxc", [DI, 16], F32, kind="ExternalInput").ap()
    wdt = nc.dram_tensor("wdt", [64, DI], F32, kind="ExternalInput").ap()
    bdtd = nc.dram_tensor("bdt", [128, DIT], F32, kind="ExternalInput").ap()
    uo = nc.dram_tensor("uo", [DI, TC], F32, kind="ExternalOutput").ap()
    deltao = nc.dram_tensor("deltao", [DI, TC], F32, kind="ExternalOutput").ap()
    Bo = nc.dram_tensor("Bo", [16, TC], F32, kind="ExternalOutput").ap()
    Co = nc.dram_tensor("Co", [16, TC], F32, kind="ExternalOutput").ap()
    sro = nc.dram_tensor("sro", [DI, TC], F32, kind="ExternalOutput").ap()

    Xb, XC = ctx.tiles("xc", DT, 512, F32)
    XNb, XN = ctx.tiles("xn", DT, TH2, BF16)
    UBc, UBC = ctx.tiles("ubc", DIT, 512, BF16)
    CW = sb("cw", [128, 4 * DIT], F32)
    CB = sb("cb", [128, DIT], F32)
    BDT = sb("bdt_t", [128, DIT], F32)
    xitile = sb("xitile", [128, TH2], F32)
    cvtile = sb("cvtile", [128, TC], F32)
    utile = sb("utile", [128, TC], F32)
    drT = sb("drt", [64, TC], BF16)
    BfC = sb("bfc", [16, 2 * TC], F32)
    scr = sb("scr", [128, 1024], F32)
    srow = sb("srow", [1, 2048], F32)
    s1 = sb("s1b", [128, 512], F32)
    s2 = sb("s2b", [128, 512], F32)
    ob = sb("ob2", [128, 512], F32)
    wbuf = sb("wbuf", [128, 2 * D], BF16)
    wdtbuf = sb("wdtbuf", [64, DI], BF16)

    psA, psB, psS, psO, psX, psM = ctx.ps

    wready = {}
    wfree = {}
    seq = 0

    def load_w(wdram, col0, mt, s, kt_n=DT):
        @bk.gpsimd
        def _(g, wdram=wdram, col0=col0, mt=mt, s=s, kt_n=kt_n):
            if s >= 2:
                g.wait_ge(sm["mm"].h, wfree[s - 2])
            g.dma_start(
                out=wbuf[:, (s % 2) * D:(s % 2) * D + kt_n * 128].rearrange(
                    "p (a q) -> p a q", a=kt_n),
                in_=wdram[:, col0 + mt * 128:col0 + (mt + 1) * 128].rearrange(
                    "(a p) m -> p a m", p=128),
            ).then_inc(sm["wl"].h, 16)
            sm["wl"].n += 16
        wready[s] = sm["wl"].n
        return wbuf[:, (s % 2) * D:(s % 2) * D + kt_n * 128]

    with nc.Block() as bk:
        @bk.vector
        def _(v):
            v.memset(ctx.ones_col[:, :], 1.0)
            v.memset(ctx.ones_row[:, :], 1.0)
            v.memset(ctx.flushbuf[:, :], 1.0)
            sm["vec"].inc(v.memset(ctx.ones_row_bf[:, :], 1.0))

        @bk.sync
        def _(s):
            s.dma_start(out=CW[:, :], in_=convw).then_inc(sm["ld"].h, 16)
            s.dma_start(out=CB[:, :], in_=convb).then_inc(sm["ld"].h, 16)
            s.dma_start(out=BDT[:, :], in_=bdtd).then_inc(sm["ld"].h, 16)
        sm["ld"].n += 48

        @bk.gpsimd
        def _(g):
            g.dma_start(out=wdtbuf[:, :], in_=wdt).then_inc(sm["wl"].h, 16)
            sm["wl"].n += 16
        wdt_wl = sm["wl"].n

        for c in range(TH2 // 512):
            t0 = c * 512

            @bk.sync
            def _(s, t0=t0):
                s.wait_ge(sm["vec"].h, sm["vec"].n)
                dma_tiles(s, sm["ld"], Xb, DT, xh[:, t0:t0 + 512])
            ld_now = sm["ld"].n

            @bk.tensor
            def _(t, ld_now=ld_now):
                t.wait_ge(sm["ld"].h, ld_now)

            @bk.scalar
            def _(sc, ld_now=ld_now):
                sc.wait_ge(sm["ld"].h, ld_now)

            emit_ln_chunk(ctx, bk, XC, XN, 0, t0, 512, psA, psB, scr, srow)
        ln_done = sm["vec"].n

        for mt in range(DIT):
            for c in range(TH2 // 512):
                wv = load_w(winT, 0, mt, seq) if c == 0 else wv

                @bk.tensor
                def _(t, s=seq, c=c, mt=mt, wv=wv):
                    if c == 0:
                        t.wait_ge(sm["wl"].h, wready[s])
                        if s == 0:
                            t.wait_ge(sm["vec"].h, ln_done)
                    t.wait_ge(sm["vec"].h, sm["vec"].n)
                    for kt in range(DT):
                        sm["mm"].inc(t.matmul(
                            psS[:, 0:512], wv[:, kt * 128:(kt + 1) * 128],
                            XN[kt][:, c * 512:(c + 1) * 512],
                            start=(kt == 0), stop=(kt == DT - 1)))

                @bk.vector
                def _(v, c=c):
                    v.wait_ge(sm["mm"].h, sm["mm"].n)
                    sm["vec"].inc(v.tensor_copy(xitile[:, c * 512:(c + 1) * 512],
                                                psS[:, 0:512]))
            wfree[seq] = sm["mm"].n
            seq += 1

            for c in range(TC // 512):
                @bk.vector
                def _(v, mt=mt, c=c):
                    if mt == 0 and c == 0:
                        v.wait_ge(sm["ld"].h, 48)
                    if c == 0:
                        v.wait_ge(sm["sc"].h, sm["sc"].n)
                    o = H2A + c * 512
                    v.tensor_single_scalar(cvtile[:, c * 512:(c + 1) * 512],
                                           xitile[:, o - 3:o - 3 + 512],
                                           CW[:, mt * 4:mt * 4 + 1], ALU.mult)
                    for k in range(1, 4):
                        v.scalar_tensor_tensor(
                            cvtile[:, c * 512:(c + 1) * 512],
                            xitile[:, o - 3 + k:o - 3 + k + 512],
                            CW[:, mt * 4 + k:mt * 4 + k + 1],
                            cvtile[:, c * 512:(c + 1) * 512], ALU.mult, ALU.add)
                    sm["vec"].inc(v.tensor_scalar_add(
                        cvtile[:, c * 512:(c + 1) * 512],
                        cvtile[:, c * 512:(c + 1) * 512], CB[:, mt:mt + 1]))

            @bk.scalar
            def _(sc, mt=mt):
                sc.wait_ge(sm["vec"].h, sm["vec"].n)
                sc.wait_ge(sm["st"].h, sm["st"].n)
                sm["sc"].inc(sc.activation(utile[:, 0:TC], cvtile[:, 0:TC], AF.Silu))

            @bk.sync
            def _(s, mt=mt):
                s.wait_ge(sm["sc"].h, sm["sc"].n)
                s.dma_start(out=uo[mt * 128:(mt + 1) * 128, :], in_=utile[:, 0:TC]
                            ).then_inc(sm["st"].h, 16)
            sm["st"].n += 16

        u_st = sm["st"].n

        for mt in range(DIT):
            wv = load_w(winT, DI, mt, seq)

            for c in range(TC // 512):
                @bk.tensor
                def _(t, s=seq, c=c, wv=wv):
                    if c == 0:
                        t.wait_ge(sm["wl"].h, wready[s])
                    t.wait_ge(sm["sc"].h, sm["sc"].n)
                    for kt in range(DT):
                        sm["mm"].inc(t.matmul(
                            psO[:, 0:512], wv[:, kt * 128:(kt + 1) * 128],
                            XN[kt][:, H2A + c * 512:H2A + (c + 1) * 512],
                            start=(kt == 0), stop=(kt == DT - 1)))

                @bk.scalar
                def _(sc, c=c):
                    sc.wait_ge(sm["mm"].h, sm["mm"].n)
                    sc.wait_ge(sm["st"].h, sm["st"].n)
                    sm["sc"].inc(sc.activation(ob[:, 0:512], psO[:, 0:512], AF.Silu))
                scn = sm["sc"].n

                @bk.sync
                def _(s, mt=mt, c=c, scn=scn):
                    s.wait_ge(sm["sc"].h, scn)
                    s.dma_start(out=sro[mt * 128:(mt + 1) * 128,
                                        c * 512:(c + 1) * 512],
                                in_=ob[:, 0:512]).then_inc(sm["st"].h, 16)
                sm["st"].n += 16
            wfree[seq] = sm["mm"].n
            seq += 1

        wxbuf = sb("wxbuf", [128, DIT * 96], BF16)

        @bk.gpsimd
        def _(g):
            g.wait_ge(sm["mm"].h, sm["mm"].n)
            g.dma_start(out=wxbuf[:, 0:DIT * 64].rearrange("p (a q) -> p a q", a=DIT),
                        in_=wxa.rearrange("(a p) m -> p a m", p=128)
                        ).then_inc(sm["wl"].h, 16)
            g.dma_start(out=wxbuf[:, DIT * 64:DIT * 80].rearrange(
                "p (a q) -> p a q", a=DIT),
                in_=wxb.rearrange("(a p) m -> p a m", p=128)).then_inc(sm["wl"].h, 16)
            g.dma_start(out=wxbuf[:, DIT * 80:DIT * 96].rearrange(
                "p (a q) -> p a q", a=DIT),
                in_=wxc.rearrange("(a p) m -> p a m", p=128)).then_inc(sm["wl"].h, 16)
            sm["wl"].n += 48
        wx_wl = sm["wl"].n

        for c in range(TC // 512):
            @bk.gpsimd
            def _(g, c=c):
                g.wait_ge(sm["st"].h, u_st)
                g.wait_ge(sm["mm"].h, sm["mm"].n)
                g.dma_start(out=UBc.rearrange("p (a t) -> p a t", a=DIT),
                            in_=uo[:, c * 512:(c + 1) * 512].rearrange(
                                "(a p) t -> p a t", p=128)).then_inc(sm["wl"].h, 16)
                sm["wl"].n += 16

            @bk.tensor
            def _(t, c=c):
                t.wait_ge(sm["wl"].h, sm["wl"].n)
                t.wait_ge(sm["vec"].h, sm["vec"].n)
                for kt in range(DIT):
                    sm["mm"].inc(t.matmul(
                        psS[0:64, 0:512], wxbuf[:, kt * 64:(kt + 1) * 64],
                        UBC[kt][:, 0:512],
                        start=(kt == 0), stop=(kt == DIT - 1)))
                for kt in range(DIT):
                    sm["mm"].inc(t.matmul(
                        psM[0:16, 0:512],
                        wxbuf[:, DIT * 64 + kt * 16:DIT * 64 + (kt + 1) * 16],
                        UBC[kt][:, 0:512],
                        start=(kt == 0), stop=(kt == DIT - 1)))
                for kt in range(DIT):
                    sm["mm"].inc(t.matmul(
                        psX[0:16, 0:512],
                        wxbuf[:, DIT * 80 + kt * 16:DIT * 80 + (kt + 1) * 16],
                        UBC[kt][:, 0:512],
                        start=(kt == 0), stop=(kt == DIT - 1)))

            @bk.vector
            def _(v, c=c):
                v.wait_ge(sm["mm"].h, sm["mm"].n)
                v.tensor_copy(drT[0:64, c * 512:(c + 1) * 512], psS[0:64, 0:512])
                v.tensor_copy(BfC[0:16, c * 512:(c + 1) * 512], psM[0:16, 0:512])
                sm["vec"].inc(v.tensor_copy(BfC[0:16, TC + c * 512:TC + (c + 1) * 512],
                                            psX[0:16, 0:512]))
        dbc_done = sm["vec"].n

        @bk.sync
        def _(s, dbc_done=dbc_done):
            s.wait_ge(sm["vec"].h, dbc_done)
            s.dma_start(out=Bo, in_=BfC[0:16, 0:TC]).then_inc(sm["st"].h, 16)
            s.dma_start(out=Co, in_=BfC[0:16, TC:2 * TC]).then_inc(sm["st"].h, 16)
        sm["st"].n += 32

        for mt in range(DIT):
            for c in range(TC // 512):
                @bk.tensor
                def _(t, mt=mt, c=c):
                    t.wait_ge(sm["wl"].h, wdt_wl)
                    if mt == 0 and c == 0:
                        t.wait_ge(sm["vec"].h, dbc_done)
                    t.wait_ge(sm["vec"].h, sm["vec"].n)
                    t.wait_ge(sm["sc"].h, sm["sc"].n)
                    sm["mm"].inc(t.matmul(psB[:, 0:512],
                                          wdtbuf[:, mt * 128:(mt + 1) * 128],
                                          drT[0:64, c * 512:(c + 1) * 512],
                                          start=True, stop=True))

                @bk.scalar
                def _(sc, mt=mt):
                    sc.wait_ge(sm["mm"].h, sm["mm"].n)
                    sc.wait_ge(sm["vec"].h, sm["vec"].n)
                    sc.activation(s1[:, 0:512], psB[:, 0:512], AF.Abs,
                                  bias=BDT[:, mt:mt + 1])
                    sc.activation(s1[:, 0:512], s1[:, 0:512], AF.Exp, scale=-1.0)
                    sm["sc"].inc(sc.activation(s1[:, 0:512], s1[:, 0:512],
                                               AF.Ln, bias=1.0))

                @bk.vector
                def _(v, mt=mt, c=c):
                    v.wait_ge(sm["sc"].h, sm["sc"].n)
                    v.wait_ge(sm["st"].h, sm["st"].n)
                    v.tensor_scalar_add(s2[:, 0:512], psB[:, 0:512], BDT[:, mt:mt + 1])
                    v.tensor_scalar_max(s2[:, 0:512], s2[:, 0:512], 0.0)
                    sm["vec"].inc(v.tensor_add(s2[:, 0:512], s2[:, 0:512],
                                               s1[:, 0:512]))
                vn = sm["vec"].n

                @bk.sync
                def _(s, mt=mt, c=c, vn=vn):
                    s.wait_ge(sm["vec"].h, vn)
                    s.dma_start(out=deltao[mt * 128:(mt + 1) * 128,
                                           c * 512:(c + 1) * 512],
                                in_=s2[:, 0:512]).then_inc(sm["st"].h, 16)
                sm["st"].n += 16

        @bk.sync
        def _(s):
            s.wait_ge(sm["st"].h, sm["st"].n)

    ctx.close()
    return nc


def host_l2a_inputs(x3, W_in, conv_w, conv_b, W_x, W_dt, b_dt):
    winT = np.ascontiguousarray(W_in.astype(np.float32))
    cw = np.zeros((128, 4 * DIT), np.float32)
    cb = np.zeros((128, DIT), np.float32)
    bdt = np.zeros((128, DIT), np.float32)
    for mt in range(DIT):
        for k in range(4):
            cw[:, mt * 4 + k] = conv_w[mt * 128:(mt + 1) * 128, 0, k]
        cb[:, mt] = conv_b[mt * 128:(mt + 1) * 128]
        bdt[:, mt] = b_dt[mt * 128:(mt + 1) * 128]
    wxa = np.ascontiguousarray(W_x[:, 0:64].astype(np.float32))
    wxb = np.ascontiguousarray(W_x[:, 64:80].astype(np.float32))
    wxc = np.ascontiguousarray(W_x[:, 80:96].astype(np.float32))
    feeds = []
    for core in range(8):
        b, q = core // 4, core % 4
        t0 = q * TC
        xh = np.zeros((D, TH2), np.float32)
        lo = max(0, t0 - H2A)
        xh[:, H2A - (t0 - lo):] = x3[b, lo:t0 + TC, :].T
        feeds.append({"xh": np.ascontiguousarray(xh), "winT": winT,
                      "convw": cw, "convb": cb, "wxa": wxa, "wxb": wxb,
                      "wxc": wxc, "wdt": np.ascontiguousarray(W_dt.astype(np.float32)),
                      "bdt": bdt})
    return feeds


def run_l2a(x3, w, nc=None):
    nc = nc or build_l2a()
    feeds = host_l2a_inputs(x3, w["W_in"], w["conv_w"], w["conv_b"],
                            w["W_x"], w["W_dt"], w["b_dt"])
    r = run_bass_kernel_spmd(nc, feeds, core_ids=list(range(8)))
    return r.results



def build_l2b():
    ctx = Ctx(psum_banks=2)
    nc = ctx.nc
    sm = ctx.sems
    sb = ctx.sb

    uTs = nc.dram_tensor("uTs", [512, S], F32, kind="ExternalInput").ap()
    dTs = nc.dram_tensor("dTs", [512, S], F32, kind="ExternalInput").ap()
    Bf = nc.dram_tensor("Bf", [16, S], F32, kind="ExternalInput").ap()
    Cf = nc.dram_tensor("Cf", [16, S], F32, kind="ExternalInput").ap()
    alog = nc.dram_tensor("alog", [128, 64], F32, kind="ExternalInput").ap()
    dcold = nc.dram_tensor("dcol", [128, 4], F32, kind="ExternalInput").ap()
    seld = nc.dram_tensor("sel16", [16, 16 * 128], F32, kind="ExternalInput").ap()
    adr = nc.dram_tensor("adr", [128, 64], F32, kind="Internal").ap()
    yo = nc.dram_tensor("yo", [512, S], F32, kind="ExternalOutput").ap()

    Ucb, UCH = ctx.tiles("uch", 4, 512, F32)
    Dcb, DCH = ctx.tiles("dch", 4, 512, F32)
    DUb, DU = ctx.tiles("du", 4, 512, F32)
    YAb, YA = ctx.tiles("ya", 4, 512, F32)
    YOb, YO = ctx.tiles("yob", 4, 512, F32)
    bcch = sb("bcch", [16, 1024], F32)
    dA = sb("dA", [128, 513], F32)
    dBu = sb("dBu", [128, 513], F32)
    hbuf = sb("hbuf", [128, 513], F32)
    Hcol = sb("Hcol", [128, 64], F32)
    ytmp = sb("ytmp", [128, 512], F32)
    ASC = sb("ASC", [128, 64], F32)
    ACOL = sb("ACOL", [128, 64], F32)
    DCOL = sb("DCOL", [128, 4], F32)
    SEL = sb("SEL", [16, 16 * 128], F32)

    psBb, psCb = ctx.ps

    with nc.Block() as bk:
        @bk.sync
        def _(s):
            s.dma_start(out=ASC[:, :], in_=alog).then_inc(sm["ld"].h, 16)
            s.dma_start(out=DCOL[:, :], in_=dcold).then_inc(sm["ld"].h, 16)
            s.dma_start(out=SEL[:, :], in_=seld).then_inc(sm["ld"].h, 16)
        sm["ld"].n += 48

        @bk.scalar
        def _(sc):
            sc.wait_ge(sm["ld"].h, sm["ld"].n)
            sm["sc"].inc(sc.activation(ASC[:, :], ASC[:, :], AF.Exp))

        @bk.vector
        def _(v):
            v.wait_ge(sm["sc"].h, sm["sc"].n)
            v.memset(Hcol[:, :], 0.0)
            v.memset(ctx.flushbuf[:, :], 1.0)
            v.memset(dA[:, 0:1], 1.0)
            v.tensor_scalar_mul(ASC[:, :], ASC[:, :], -1.0)
            sm["vec"].inc(ctx.fl(v))

        @bk.sync
        def _(s):
            s.wait_ge(sm["vec"].h, sm["vec"].n)
            s.dma_start(out=adr, in_=ASC[:, :]).then_inc(sm["st"].h, 16)
            s.wait_ge(sm["st"].h, 16)
            s.dma_start(out=ACOL[:, :], in_=adr).then_inc(sm["ld"].h, 16)
        sm["st"].n += 16
        sm["ld"].n += 16

        for ch in range(S // 512):
            t0 = ch * 512

            @bk.sync
            def _(s, t0=t0, ch=ch):
                if ch > 0:
                    s.wait_ge(sm["vec"].h, sm["vec"].n)
                    s.wait_ge(sm["mm"].h, sm["mm"].n)
                    s.wait_ge(sm["st"].h, sm["st"].n)
                s.dma_start(out=Ucb.rearrange("p (a t) -> p a t", a=4),
                            in_=uTs[:, t0:t0 + 512].rearrange("(a p) t -> p a t", p=128)
                            ).then_inc(sm["ld"].h, 16)
                s.dma_start(out=Dcb.rearrange("p (a t) -> p a t", a=4),
                            in_=dTs[:, t0:t0 + 512].rearrange("(a p) t -> p a t", p=128)
                            ).then_inc(sm["ld"].h, 16)
                s.dma_start(out=bcch[:, 0:512], in_=Bf[:, t0:t0 + 512]
                            ).then_inc(sm["ld"].h, 16)
                s.dma_start(out=bcch[:, 512:1024], in_=Cf[:, t0:t0 + 512]
                            ).then_inc(sm["ld"].h, 16)
            sm["ld"].n += 64

            @bk.vector
            def _(v, ch=ch):
                v.wait_ge(sm["ld"].h, sm["ld"].n)
                for dt in range(4):
                    ins = v.tensor_mul(DU[dt][:, :], DCH[dt][:, :], UCH[dt][:, :])
                sm["vec"].inc(ins)

            for n in range(16):
                @bk.tensor
                def _(t, n=n):
                    t.wait_ge(sm["ld"].h, sm["ld"].n)
                    t.wait_ge(sm["vec"].h, sm["vec"].n)
                    sm["mm"].inc(t.matmul(psBb[:, 0:512], SEL[:, n * 128:(n + 1) * 128],
                                          bcch[0:16, 0:512], start=True, stop=True))
                    sm["mm"].inc(t.matmul(psCb[:, 0:512], SEL[:, n * 128:(n + 1) * 128],
                                          bcch[0:16, 512:1024], start=True, stop=True))

                for dt in range(4):
                    idx = dt * 16 + n

                    @bk.scalar
                    def _(sc, dt=dt, idx=idx):
                        sc.wait_ge(sm["vec"].h, sm["vec"].n)
                        sm["sc"].inc(sc.activation(dA[:, 1:513], DCH[dt][:, :],
                                                   AF.Exp,
                                                   scale=ACOL[:, idx:idx + 1]))

                    @bk.vector
                    def _(v, dt=dt, n=n, idx=idx):
                        v.wait_ge(sm["sc"].h, sm["sc"].n)
                        if dt == 0:
                            v.wait_ge(sm["mm"].h, sm["mm"].n)
                        v.tensor_copy(dBu[:, 0:1], Hcol[:, idx:idx + 1])
                        v.tensor_mul(dBu[:, 1:513], DU[dt][:, :], psBb[:, 0:512])
                        ctx.fl(v)
                        v.tensor_tensor_scan(hbuf[:, 0:513], dA[:, 0:513],
                                             dBu[:, 0:513], 0.0, ALU.mult, ALU.add)
                        v.tensor_copy(Hcol[:, idx:idx + 1], hbuf[:, 512:513])
                        if n == 0:
                            ins = v.tensor_mul(YA[dt][:, :], hbuf[:, 1:513],
                                               psCb[:, 0:512])
                        else:
                            v.tensor_mul(ytmp[:, :], hbuf[:, 1:513], psCb[:, 0:512])
                            ins = v.tensor_add(YA[dt][:, :], YA[dt][:, :], ytmp[:, :])
                        sm["vec"].inc(ins)

            @bk.vector
            def _(v, ch=ch):
                for dt in range(4):
                    ins = v.scalar_tensor_tensor(YO[dt][:, :], UCH[dt][:, :],
                                                 DCOL[:, dt:dt + 1], YA[dt][:, :],
                                                 ALU.mult, ALU.add)
                sm["vec"].inc(ins)
            yn = sm["vec"].n

            @bk.sync
            def _(s, t0=t0, yn=yn):
                s.wait_ge(sm["vec"].h, yn)
                s.dma_start(out=yo[:, t0:t0 + 512].rearrange("(a p) t -> p a t", p=128),
                            in_=YOb.rearrange("p (a t) -> p a t", a=4)
                            ).then_inc(sm["st"].h, 16)
            sm["st"].n += 16

        @bk.sync
        def _(s):
            s.wait_ge(sm["st"].h, sm["st"].n)

    ctx.close()
    return nc


def host_l2b_inputs(resa, A_log, Dvec):
    sel = np.zeros((16, 16 * 128), np.float32)
    for n in range(16):
        sel[n, n * 128:(n + 1) * 128] = 1.0
    feeds = []
    for core in range(8):
        b, dq = core // 4, core % 4
        r0 = dq * 512
        u = np.concatenate([resa[b * 4 + q]["uo"][r0:r0 + 512, :] for q in range(4)],
                           axis=1)
        dl = np.concatenate([resa[b * 4 + q]["deltao"][r0:r0 + 512, :]
                             for q in range(4)], axis=1)
        Bc = np.concatenate([resa[b * 4 + q]["Bo"] for q in range(4)], axis=1)
        Cc = np.concatenate([resa[b * 4 + q]["Co"] for q in range(4)], axis=1)
        al = np.zeros((128, 64), np.float32)
        dc = np.zeros((128, 4), np.float32)
        for dt in range(4):
            for n in range(16):
                al[:, dt * 16 + n] = A_log[r0 + dt * 128:r0 + (dt + 1) * 128, n]
            dc[:, dt] = Dvec[r0 + dt * 128:r0 + (dt + 1) * 128]
        feeds.append({"uTs": np.ascontiguousarray(u), "dTs": np.ascontiguousarray(dl),
                      "Bf": np.ascontiguousarray(Bc), "Cf": np.ascontiguousarray(Cc),
                      "alog": al, "dcol": dc, "sel16": sel})
    return feeds


def run_l2b(resa, w, nc=None):
    nc = nc or build_l2b()
    feeds = host_l2b_inputs(resa, w["A_log"], w["D"])
    r = run_bass_kernel_spmd(nc, feeds, core_ids=list(range(8)))
    return r.results



def build_l2c():
    ctx = Ctx(psum_banks=2)
    nc = ctx.nc
    sm = ctx.sems
    sb = ctx.sb

    yts = nc.dram_tensor("yts", [DI, TC], F32, kind="ExternalInput").ap()
    srt = nc.dram_tensor("srt", [DI, TC], F32, kind="ExternalInput").ap()
    xd = nc.dram_tensor("xd", [D, TC], F32, kind="ExternalInput").ap()
    wout = nc.dram_tensor("wout", [DI, D], F32, kind="ExternalInput").ap()
    xo = nc.dram_tensor("xo", [D, TC], F32, kind="ExternalOutput").ap()

    Yb, YC = ctx.tiles("yc", DIT, 512, F32)
    Rb, RC = ctx.tiles("rc", DIT, 512, F32)
    Y2b, Y2 = ctx.tiles("y2", DIT, 512, BF16)
    Xdb, XD = ctx.tiles("xdc", DT, 512, F32)
    ob = sb("ob", [128, 512], F32)
    wob = sb("wob", [128, DIT * 128], BF16)
    wfull = sb("wfull", [128, DIT * D], BF16)

    psS, psO = ctx.ps

    with nc.Block() as bk:
        @bk.vector
        def _(v):
            sm["vec"].inc(v.memset(ctx.flushbuf[:, :], 1.0))

        @bk.gpsimd
        def _(g):
            g.dma_start(out=wfull.rearrange("p (a m) -> p a m", a=DIT),
                        in_=wout.rearrange("(a p) m -> p a m", p=128)
                        ).then_inc(sm["wl"].h, 16)
            sm["wl"].n += 16

        for c in range(NCH0):
            t0 = c * 512

            @bk.sync
            def _(s, t0=t0, c=c):
                if c > 0:
                    s.wait_ge(sm["vec"].h, sm["vec"].n)
                s.dma_start(out=Yb.rearrange("p (a t) -> p a t", a=DIT),
                            in_=yts[:, t0:t0 + 512].rearrange("(a p) t -> p a t", p=128)
                            ).then_inc(sm["ld"].h, 16)
                s.dma_start(out=Rb.rearrange("p (a t) -> p a t", a=DIT),
                            in_=srt[:, t0:t0 + 512].rearrange("(a p) t -> p a t", p=128)
                            ).then_inc(sm["ld"].h, 16)
                s.dma_start(out=Xdb.rearrange("p (a t) -> p a t", a=DT),
                            in_=xd[:, t0:t0 + 512].rearrange("(a p) t -> p a t", p=128)
                            ).then_inc(sm["ld"].h, 16)
            sm["ld"].n += 48

            @bk.vector
            def _(v, c=c):
                v.wait_ge(sm["ld"].h, sm["ld"].n)
                if c == 0:
                    v.wait_ge(sm["wl"].h, sm["wl"].n)
                for kt in range(DIT):
                    ins = v.tensor_mul(Y2[kt][:, :], YC[kt][:, :], RC[kt][:, :])
                sm["vec"].inc(ins)
            y2n = sm["vec"].n

            for mt in range(DT):
                @bk.tensor
                def _(t, mt=mt, c=c, y2n=y2n):
                    t.wait_ge(sm["vec"].h, y2n)
                    t.wait_ge(sm["vec"].h, sm["vec"].n)
                    for kt in range(DIT):
                        sm["mm"].inc(t.matmul(
                            psS[:, 0:512],
                            wfull[:, kt * D + mt * 128:kt * D + (mt + 1) * 128],
                            Y2[kt][:, :], start=(kt == 0), stop=(kt == DIT - 1)))

                @bk.vector
                def _(v, mt=mt):
                    v.wait_ge(sm["mm"].h, sm["mm"].n)
                    v.wait_ge(sm["st"].h, sm["st"].n)
                    sm["vec"].inc(v.tensor_add(ob[:, :], psS[:, 0:512], XD[mt][:, :]))
                vn = sm["vec"].n

                @bk.sync
                def _(s, mt=mt, t0=t0, vn=vn):
                    s.wait_ge(sm["vec"].h, vn)
                    s.dma_start(out=xo[mt * 128:(mt + 1) * 128, t0:t0 + 512],
                                in_=ob[:, :]).then_inc(sm["st"].h, 16)
                sm["st"].n += 16

        @bk.sync
        def _(s):
            s.wait_ge(sm["st"].h, sm["st"].n)

    ctx.close()
    return nc


def host_l2c_inputs(resa, resb, x3, W_out):
    wo = np.ascontiguousarray(W_out.astype(np.float32))
    feeds = []
    for core in range(8):
        b, q = core // 4, core % 4
        yfull = np.concatenate([resb[b * 4 + d]["yo"] for d in range(4)], axis=0)
        yts = np.ascontiguousarray(yfull[:, q * TC:(q + 1) * TC])
        feeds.append({"yts": yts,
                      "srt": np.ascontiguousarray(resa[core]["sro"]),
                      "xd": np.ascontiguousarray(x3[b, q * TC:(q + 1) * TC, :].T),
                      "wout": wo})
    return feeds


def run_l2c(resa, resb, x3, w, nc=None):
    nc = nc or build_l2c()
    feeds = host_l2c_inputs(resa, resb, x3, w["W_out"])
    r = run_bass_kernel_spmd(nc, feeds, core_ids=list(range(8)))
    out = np.zeros((2, S, D), np.float32)
    for core in range(8):
        b, q = core // 4, core % 4
        out[b, q * TC:(q + 1) * TC, :] = r.results[core]["xo"].T
    return out



H1F = 640
T1F = TC + H1F
KSP = 2560
QSP = 2304
NB1 = 9
FH = 2816
FHT = FH // 128


def build_l1f():
    ctx = Ctx(psum_banks=8)
    nc = ctx.nc
    sm = ctx.sems
    sb = ctx.sb

    xh = nc.dram_tensor("xh", [D, T1F], F32, kind="ExternalInput").ap()
    wqkvT = nc.dram_tensor("wqkvT", [D, 3 * D], F32, kind="ExternalInput").ap()
    woT = nc.dram_tensor("woT", [D, D], F32, kind="ExternalInput").ap()
    w1a = nc.dram_tensor("w1a", [D, FH], F32, kind="ExternalInput").ap()
    w1g = nc.dram_tensor("w1g", [D, FH], F32, kind="ExternalInput").ap()
    w2p = nc.dram_tensor("w2p", [FH, D], F32, kind="ExternalInput").ap()
    b1ac = nc.dram_tensor("b1ac", [128, FHT], F32, kind="ExternalInput").ap()
    b1gc = nc.dram_tensor("b1gc", [128, FHT], F32, kind="ExternalInput").ap()
    b2c = nc.dram_tensor("b2c", [128, DT], F32, kind="ExternalInput").ap()
    glnc = nc.dram_tensor("glnc", [128, DT], F32, kind="ExternalInput").ap()
    blnc = nc.dram_tensor("blnc", [128, DT], F32, kind="ExternalInput").ap()
    qk2d = nc.dram_tensor("qk2", [128, 2], F32, kind="ExternalInput").ap()
    rotd = nc.dram_tensor("rotq", [128, 128], F32, kind="ExternalInput").ap()
    hsad = nc.dram_tensor("hsa", [128, 2], F32, kind="ExternalInput").ap()
    hsbd = nc.dram_tensor("hsb", [2, 128], F32, kind="ExternalInput").ap()
    idd = nc.dram_tensor("idm", [128, 128], F32, kind="ExternalInput").ap()
    tabd = nc.dram_tensor("tabs", [128, 4 * KSP], F32, kind="ExternalInput").ap()
    mqbd = nc.dram_tensor("mqb", [128, NB1 * 2 * 512], BF16, kind="ExternalInput").ap()
    mktd = nc.dram_tensor("mkt", [128, NB1 * 4 * 256], BF16, kind="ExternalInput").ap()
    qdram = nc.dram_tensor("qdram", [D, KSP], BF16, kind="Internal").ap()
    kdram = nc.dram_tensor("kdram", [D, KSP], BF16, kind="Internal").ap()
    vdram = nc.dram_tensor("vdram", [KSP, D], BF16, kind="Internal").ap()
    ydram = nc.dram_tensor("ydram", [D, QSP], F32, kind="Internal").ap()
    xo = nc.dram_tensor("xo", [D, TC], F32, kind="ExternalOutput").ap()

    XW = 2564
    arena, XBt = ctx.tiles("arena1", DT, XW, BF16)
    XB = [XBt[i][:, 0:2561] for i in range(DT)]
    OT = [arena[:, i * QSP:(i + 1) * QSP] for i in range(DT)]
    af32 = arena.bitcast(F32)
    YH = [af32[:, i * 513:(i + 1) * 513] for i in range(DT)]
    SXC = [af32[:, 4104 + i * 512:4104 + (i + 1) * 512] for i in range(DT)]
    XS = [arena[:, 16400 + i * 512:16400 + (i + 1) * 512] for i in range(DT)]

    TAB = sb("tabs_s", [128, 4 * KSP], BF16)
    COSS = TAB[:, 0:KSP]
    SINS = TAB[:, KSP:2 * KSP]
    COSI = TAB[:, 2 * KSP:3 * KSP]
    SINI = TAB[:, 3 * KSP:4 * KSP]
    WV = sb("wv1", [128, DT * D], BF16)
    KS = sb("ks1", [64, KSP], BF16)
    QS = sb("qs1", [64, KSP], BF16)
    VS = sb("vs1", [128, 20 * 64], BF16)
    MQ = sb("mq1", [128, NB1 * 2 * 512], BF16)
    MK = sb("mk1", [128, NB1 * 4 * 256], BF16)
    Hb, HT = ctx.tiles("h1", FHT, 512, BF16)
    wbuf = sb("wbuf1", [128, 4096], BF16)
    wbuf2 = sb("wbuf21", [128, 2 * FH], BF16)
    t1 = sb("t1b", [128, 512], F32)
    t2 = sb("t2b", [128, 512], F32)
    u1 = sb("u1b", [128, 512], F32)
    s1 = sb("s1c", [128, 512], F32)
    s2 = sb("s2c", [128, 512], F32)
    qtmp = sb("qtmp", [128, 512], BF16)
    PT = sb("pt1", [128, 1024], BF16)
    vtmp = sb("vtmp", [128, 1024], BF16)
    Mb = sb("mbb", [128, 256], F32)
    rb = sb("rbb", [64, 256], F32)
    mc2 = sb("mc2b", [128, 2], F32)
    mrowb = sb("mrowb", [1, 256], F32)
    lrow = sb("lrowb", [1, 256], F32)
    rnr = sb("rnr", [2, 512], F32)
    srow = sb("srowb", [1, 1536], F32)
    scr = sb("scrb", [128, 512], F32)
    xrb = sb("xrb", [128, 512], F32)
    QK2 = sb("qk2s", [128, 2], F32)
    ROT = sb("rots", [128, 128], F32)
    HSA = sb("hsas", [128, 2], F32)
    HSB = sb("hsbs", [2, 128], F32)
    IDM = sb("idms", [128, 128], F32)
    B1A = sb("b1as", [128, FHT], F32)
    B1G = sb("b1gs", [128, FHT], F32)
    B2C = sb("b2cs", [128, DT], F32)
    GLN = sb("glns", [128, DT], F32)
    BLN = sb("blns", [128, DT], F32)
    ones_col_bf = sb("onescbf", [128, 1], BF16)

    psA, psB, psC, psQK, psM, psS, psO, psR = ctx.ps

    def xsl(kt, j0, w):
        o = j0 + (1 if kt < 4 else 0)
        return XB[kt][:, o:o + w]

    wready = {}
    wfree = {}

    with nc.Block() as bk:
        @bk.vector
        def _(v):
            v.memset(ctx.ones_col[:, :], 1.0)
            v.memset(ctx.ones_row[:, :], 1.0)
            v.memset(ctx.ones_row_bf[:, :], 1.0)
            v.memset(ones_col_bf[:, :], 1.0)
            sm["vec"].inc(v.memset(ctx.flushbuf[:, :], 1.0))

        @bk.sync
        def _(s):
            s.dma_start(out=QK2[:, :], in_=qk2d).then_inc(sm["ld"].h, 16)
            s.dma_start(out=ROT[:, :], in_=rotd).then_inc(sm["ld"].h, 16)
            s.dma_start(out=HSA[:, :], in_=hsad).then_inc(sm["ld"].h, 16)
            s.dma_start(out=HSB[:, :], in_=hsbd).then_inc(sm["ld"].h, 16)
            s.dma_start(out=IDM[:, :], in_=idd).then_inc(sm["ld"].h, 16)
            s.dma_start(out=MQ[:, :], in_=mqbd).then_inc(sm["ld"].h, 16)
            s.dma_start(out=MK[:, :], in_=mktd).then_inc(sm["ld"].h, 16)
            s.dma_start(out=B1A[:, :], in_=b1ac).then_inc(sm["ld"].h, 16)
            s.dma_start(out=B1G[:, :], in_=b1gc).then_inc(sm["ld"].h, 16)
            s.dma_start(out=B2C[:, :], in_=b2c).then_inc(sm["ld"].h, 16)
            s.dma_start(out=GLN[:, :], in_=glnc).then_inc(sm["ld"].h, 16)
            s.dma_start(out=BLN[:, :], in_=blnc).then_inc(sm["ld"].h, 16)
        sm["ld"].n += 192

        @bk.gpsimd
        def _(g):
            g.dma_start(out=TAB.rearrange("p (a t) -> p a t", a=4),
                        in_=tabd.rearrange("p (a t) -> p a t", a=4)
                        ).then_inc(sm["wl"].h, 16)
            for kt in range(DT):
                g.dma_start(out=XB[kt], in_=xh[kt * 128:(kt + 1) * 128, 127:2688]
                            ).then_inc(sm["wl"].h, 16)
            g.dma_start(out=WV.rearrange("p (a m) -> p a m", a=DT),
                        in_=wqkvT[:, 2 * D:3 * D].rearrange("(a p) m -> p a m", p=128)
                        ).then_inc(sm["wl"].h, 16)
            sm["wl"].n += 160

        seq = 0
        for i in range(16):
            isq = i < 8

            @bk.gpsimd
            def _(g, i=i, s=seq):
                if s >= 2:
                    g.wait_ge(sm["mm"].h, wfree[s - 2])
                g.dma_start(
                    out=wbuf[:, (s % 2) * 1024:(s % 2) * 1024 + 1024].rearrange(
                        "p (a q) -> p a q", a=DT),
                    in_=wqkvT[:, i * 128:(i + 1) * 128].rearrange(
                        "(a p) m -> p a m", p=128)).then_inc(sm["wl"].h, 16)
                sm["wl"].n += 16
            wready[seq] = sm["wl"].n
            wv = wbuf[:, (seq % 2) * 1024:(seq % 2) * 1024 + 1024]

            for c in range(5):
                @bk.tensor
                def _(t, s=seq, c=c, wv=wv):
                    t.wait_ge(sm["wl"].h, wready[s])
                    t.wait_ge(sm["vec"].h, sm["vec"].n)
                    for kt in range(DT):
                        sm["mm"].inc(t.matmul(psA[:, 0:512],
                                              wv[:, kt * 128:(kt + 1) * 128],
                                              xsl(kt, c * 512, 512),
                                              start=(kt == 0), stop=(kt == DT - 1)))

                @bk.vector
                def _(v):
                    v.wait_ge(sm["mm"].h, sm["mm"].n)
                    sm["vec"].inc(v.tensor_copy(t1[:, :], psA[:, 0:512]))

                @bk.scalar
                def _(sc):
                    sc.wait_ge(sm["vec"].h, sm["vec"].n)
                    sm["sc"].inc(sc.activation(s1[:, :], t1[:, :], AF.Square))

                @bk.tensor
                def _(t):
                    t.wait_ge(sm["sc"].h, sm["sc"].n)
                    sm["mm"].inc(t.matmul(psB[0:2, 0:512], HSA[:, :], s1[:, :],
                                          start=True, stop=True))

                @bk.vector
                def _(v):
                    v.wait_ge(sm["mm"].h, sm["mm"].n)
                    sm["vec"].inc(v.tensor_scalar_max(rnr[0:2, :], psB[0:2, 0:512],
                                                      1e-24))

                @bk.scalar
                def _(sc):
                    sc.wait_ge(sm["vec"].h, sm["vec"].n)
                    sc.activation(rnr[0:2, :], rnr[0:2, :], AF.Ln)
                    sm["sc"].inc(sc.activation(rnr[0:2, :], rnr[0:2, :],
                                               AF.Exp, scale=-0.5))

                @bk.tensor
                def _(t):
                    t.wait_ge(sm["sc"].h, sm["sc"].n)
                    sm["mm"].inc(t.matmul(psB[:, 0:512], HSB[:, :], rnr[0:2, :],
                                          start=True, stop=True))

                @bk.vector
                def _(v, isq=isq):
                    v.wait_ge(sm["mm"].h, sm["mm"].n)
                    v.tensor_mul(t2[:, :], t1[:, :], psB[:, 0:512])
                    sm["vec"].inc(v.tensor_single_scalar(
                        t2[:, :], t2[:, :], QK2[:, 0:1] if isq else QK2[:, 1:2],
                        ALU.mult))

                @bk.tensor
                def _(t):
                    t.wait_ge(sm["vec"].h, sm["vec"].n)
                    sm["mm"].inc(t.matmul(psC[:, 0:512], ROT[:, :], t2[:, :],
                                          start=True, stop=True))

                @bk.vector
                def _(v, c=c, isq=isq):
                    v.wait_ge(sm["mm"].h, sm["mm"].n)
                    v.wait_ge(sm["st"].h, sm["st"].n)
                    CO = COSS if isq else COSI
                    SI = SINS if isq else SINI
                    v.tensor_mul(u1[:, :], t2[:, :], CO[:, c * 512:(c + 1) * 512])
                    v.tensor_mul(s2[:, :], psC[:, 0:512],
                                 SI[:, c * 512:(c + 1) * 512])
                    sm["vec"].inc(v.tensor_add(qtmp[:, :], u1[:, :], s2[:, :]))
                vn = sm["vec"].n

                @bk.sync
                def _(s, i=i, c=c, vn=vn, isq=isq):
                    s.wait_ge(sm["vec"].h, vn)
                    dst = qdram if isq else kdram
                    r0 = (i % 8) * 128
                    s.dma_start(out=dst[r0:r0 + 128, c * 512:(c + 1) * 512],
                                in_=qtmp[:, :]).then_inc(sm["st"].h, 16)
                sm["st"].n += 16
            wfree[seq] = sm["mm"].n
            seq += 1

        for tt in range(20):
            @bk.tensor
            def _(t, tt=tt):
                t.wait_ge(sm["wl"].h, sm["wl"].n)
                t.wait_ge(sm["vec"].h, sm["vec"].n)
                for half in range(2):
                    ps = psA if half == 0 else psC
                    for kt in range(DT):
                        sm["mm"].inc(t.matmul(
                            ps[:, 0:512],
                            xsl(kt, tt * 128, 128),
                            WV[:, kt * D + half * 512:kt * D + half * 512 + 512],
                            start=(kt == 0), stop=(kt == DT - 1)))

            @bk.vector
            def _(v, tt=tt):
                v.wait_ge(sm["mm"].h, sm["mm"].n)
                v.wait_ge(sm["st"].h, sm["st"].n)
                v.tensor_copy(vtmp[:, 0:512], psA[:, 0:512])
                sm["vec"].inc(v.tensor_copy(vtmp[:, 512:1024], psC[:, 0:512]))
            vn = sm["vec"].n

            @bk.sync
            def _(s, tt=tt, vn=vn):
                s.wait_ge(sm["vec"].h, vn)
                s.dma_start(out=vdram[tt * 128:(tt + 1) * 128, :], in_=vtmp[:, :]
                            ).then_inc(sm["st"].h, 16)
            sm["st"].n += 16
        qkv_st = sm["st"].n

        for h in range(16):
            @bk.sync
            def _(s, h=h):
                if h == 0:
                    s.wait_ge(sm["st"].h, qkv_st)
                else:
                    s.wait_ge(sm["mm"].h, sm["mm"].n)
                s.dma_start(out=KS[:, :], in_=kdram[h * 64:(h + 1) * 64, :]
                            ).then_inc(sm["ld"].h, 16)
                s.dma_start(out=QS[:, :], in_=qdram[h * 64:(h + 1) * 64, :]
                            ).then_inc(sm["ld"].h, 16)
                s.dma_start(out=VS.rearrange("p (a q) -> p a q", a=20),
                            in_=vdram[:, h * 64:(h + 1) * 64].rearrange(
                                "(a p) q -> p a q", p=128)).then_inc(sm["ld"].h, 16)
            sm["ld"].n += 48

            for c in range(NB1):
                qc0 = 256 + c * 256
                kc0 = c * 256

                for qt in range(2):
                    @bk.tensor
                    def _(t, qt=qt, qc0=qc0, kc0=kc0, c=c, h=h):
                        t.wait_ge(sm["ld"].h, sm["ld"].n)
                        t.wait_ge(sm["vec"].h, sm["vec"].n)
                        sm["mm"].inc(t.matmul(
                            psQK[:, 0:512], QS[:, qc0 + qt * 128:qc0 + qt * 128 + 128],
                            KS[:, kc0:kc0 + 512], start=True, stop=True))

                    @bk.vector
                    def _(v, qt=qt, c=c):
                        v.wait_ge(sm["mm"].h, sm["mm"].n)
                        v.tensor_add(s1[:, :], psQK[:, 0:512],
                                     MQ[:, (c * 2 + qt) * 512:(c * 2 + qt + 1) * 512])
                        ins = v.reduce_max(mc2[:, qt:qt + 1], s1[:, :], axis=AX.X)
                        if qt == 1:
                            ins = ctx.fl(v)
                        sm["vec"].inc(ins)

                @bk.tensor
                def _(t):
                    t.wait_ge(sm["vec"].h, sm["vec"].n)
                    t.matmul(psM[0:1, 0:128], mc2[:, 0:1], IDM[:, :],
                             start=True, stop=True)
                    sm["mm"].inc(t.matmul(psM[0:1, 128:256], mc2[:, 1:2], IDM[:, :],
                                          start=True, stop=True))

                @bk.vector
                def _(v):
                    v.wait_ge(sm["mm"].h, sm["mm"].n)
                    v.tensor_copy(mrowb[0:1, :], psM[0:1, 0:256])
                    sm["vec"].inc(ctx.fl(v))

                @bk.tensor
                def _(t):
                    t.wait_ge(sm["vec"].h, sm["vec"].n)
                    sm["mm"].inc(t.matmul(psM[:, 256:512], ctx.ones_row[:, :],
                                          mrowb[0:1, :], start=True, stop=True))

                @bk.vector
                def _(v):
                    v.wait_ge(sm["mm"].h, sm["mm"].n)
                    sm["vec"].inc(v.tensor_copy(Mb[:, :], psM[:, 256:512]))

                for kt in range(4):
                    @bk.tensor
                    def _(t, kt=kt, qc0=qc0, kc0=kc0):
                        t.wait_ge(sm["vec"].h, sm["vec"].n)
                        t.wait_ge(sm["sc"].h, sm["sc"].n)
                        sm["mm"].inc(t.matmul(
                            psS[:, 0:256], KS[:, kc0 + kt * 128:kc0 + kt * 128 + 128],
                            QS[:, qc0:qc0 + 256], start=True, stop=True))

                    @bk.vector
                    def _(v, kt=kt):
                        v.wait_ge(sm["mm"].h, sm["mm"].n)
                        v.tensor_sub(s1[:, 0:256], psS[:, 0:256], Mb[:, :])
                        sm["vec"].inc(v.tensor_scalar_min(s1[:, 0:256], s1[:, 0:256],
                                                          0.0))

                    @bk.scalar
                    def _(sc, kt=kt):
                        sc.wait_ge(sm["vec"].h, sm["vec"].n)
                        sm["sc"].inc(sc.activation(s2[:, 0:256], s1[:, 0:256],
                                                   AF.Exp, scale=8.0))

                    @bk.vector
                    def _(v, kt=kt, c=c):
                        v.wait_ge(sm["sc"].h, sm["sc"].n)
                        sm["vec"].inc(v.tensor_mul(
                            PT[:, kt * 256:(kt + 1) * 256], s2[:, 0:256],
                            MK[:, (c * 4 + kt) * 256:(c * 4 + kt + 1) * 256]))

                @bk.tensor
                def _(t, c=c):
                    t.wait_ge(sm["vec"].h, sm["vec"].n)
                    for kt in range(4):
                        vsl = VS[:, (2 * c + kt) * 64:(2 * c + kt + 1) * 64]
                        t.matmul(psO[0:64, 0:256], vsl,
                                 PT[:, kt * 256:(kt + 1) * 256],
                                 start=(kt == 0), stop=(kt == 3))
                        sm["mm"].inc(t.matmul(psR[0:1, 0:256], ones_col_bf[:, :],
                                              PT[:, kt * 256:(kt + 1) * 256],
                                              start=(kt == 0), stop=(kt == 3)))

                @bk.vector
                def _(v):
                    v.wait_ge(sm["mm"].h, sm["mm"].n)
                    v.tensor_copy(lrow[0:1, :], psR[0:1, 0:256])
                    v.tensor_scalar_add(lrow[0:1, :], lrow[0:1, :], 1e-30)
                    v.reciprocal(lrow[0:1, :], lrow[0:1, :])
                    sm["vec"].inc(ctx.fl(v))

                @bk.tensor
                def _(t):
                    t.wait_ge(sm["vec"].h, sm["vec"].n)
                    sm["mm"].inc(t.matmul(psM[0:64, 0:256], ctx.ones_row[0:1, 0:64],
                                          lrow[0:1, :], start=True, stop=True))

                @bk.vector
                def _(v, h=h, c=c):
                    v.wait_ge(sm["mm"].h, sm["mm"].n)
                    v.tensor_copy(rb[:, :], psM[0:64, 0:256])
                    sm["vec"].inc(v.tensor_mul(
                        OT[h // 2][(h % 2) * 64:(h % 2) * 64 + 64,
                                   c * 256:(c + 1) * 256],
                        psO[0:64, 0:256], rb[:, :]))

        seq2 = 0
        for mt in range(DT):
            @bk.gpsimd
            def _(g, mt=mt, s=seq2):
                if s >= 2:
                    g.wait_ge(sm["mm"].h, wfree[("o", s - 2)])
                else:
                    g.wait_ge(sm["mm"].h, wfree[15])
                g.dma_start(
                    out=wbuf[:, (s % 2) * 1024:(s % 2) * 1024 + 1024].rearrange(
                        "p (a q) -> p a q", a=DT),
                    in_=woT[:, mt * 128:(mt + 1) * 128].rearrange(
                        "(a p) m -> p a m", p=128)).then_inc(sm["wl"].h, 16)
                sm["wl"].n += 16
            wready[("o", seq2)] = sm["wl"].n
            wv = wbuf[:, (seq2 % 2) * 1024:(seq2 % 2) * 1024 + 1024]

            for oc in range(NB1):
                @bk.sync
                def _(s, mt=mt, oc=oc):
                    s.wait_ge(sm["vec"].h, sm["vec"].n)
                    s.dma_start(out=xrb[:, 0:256],
                                in_=xh[mt * 128:(mt + 1) * 128,
                                       384 + oc * 256:384 + (oc + 1) * 256]
                                ).then_inc(sm["ld"].h, 16)
                sm["ld"].n += 16

                @bk.tensor
                def _(t, s=seq2, oc=oc, wv=wv):
                    t.wait_ge(sm["wl"].h, wready[("o", s)])
                    t.wait_ge(sm["vec"].h, sm["vec"].n)
                    for kt in range(DT):
                        sm["mm"].inc(t.matmul(psA[:, 0:256],
                                              wv[:, kt * 128:(kt + 1) * 128],
                                              OT[kt][:, oc * 256:(oc + 1) * 256],
                                              start=(kt == 0), stop=(kt == DT - 1)))

                @bk.vector
                def _(v):
                    v.wait_ge(sm["mm"].h, sm["mm"].n)
                    v.wait_ge(sm["ld"].h, sm["ld"].n)
                    v.wait_ge(sm["st"].h, sm["st"].n)
                    sm["vec"].inc(v.tensor_add(t1[:, 0:256], psA[:, 0:256],
                                               xrb[:, 0:256]))
                vn = sm["vec"].n

                @bk.sync
                def _(s, mt=mt, oc=oc, vn=vn):
                    s.wait_ge(sm["vec"].h, vn)
                    s.dma_start(out=ydram[mt * 128:(mt + 1) * 128,
                                          oc * 256:(oc + 1) * 256],
                                in_=t1[:, 0:256]).then_inc(sm["st"].h, 16)
                sm["st"].n += 16
            wfree[("o", seq2)] = sm["mm"].n
            seq2 += 1
        y_st = sm["st"].n

        seq3 = 0
        for fc in range(4):
            y0 = 256 + fc * 512 - 1

            @bk.sync
            def _(s, y0=y0, fc=fc):
                s.wait_ge(sm["st"].h, y_st if fc == 0 else sm["st"].n)
                s.wait_ge(sm["mm"].h, sm["mm"].n)
                s.wait_ge(sm["vec"].h, sm["vec"].n)
                for kt in range(DT):
                    s.dma_start(out=YH[kt],
                                in_=ydram[kt * 128:(kt + 1) * 128, y0:y0 + 513]
                                ).then_inc(sm["ld"].h, 16)
            sm["ld"].n += 128

            @bk.vector
            def _(v):
                v.wait_ge(sm["ld"].h, sm["ld"].n)
                for kt in range(DT):
                    src = YH[kt][:, 1:513] if kt < 4 else YH[kt][:, 0:512]
                    ins = v.tensor_copy(SXC[kt], src)
                sm["vec"].inc(ins)

            emit_ln_gb(ctx, bk, SXC, XS, 0, 512, psA, psB, scr, srow, GLN, BLN)

            for mt in range(FHT):
                @bk.gpsimd
                def _(g, mt=mt, s=seq3, fc=fc):
                    if s >= 2:
                        g.wait_ge(sm["mm"].h, wfree[("f", s - 2)])
                    else:
                        g.wait_ge(sm["mm"].h, wfree[("o", 7)])
                    sl = (s % 2) * 2048
                    g.dma_start(
                        out=wbuf[:, sl:sl + 1024].rearrange("p (a q) -> p a q", a=DT),
                        in_=w1a[:, mt * 128:(mt + 1) * 128].rearrange(
                            "(a p) m -> p a m", p=128)).then_inc(sm["wl"].h, 16)
                    g.dma_start(
                        out=wbuf[:, sl + 1024:sl + 2048].rearrange(
                            "p (a q) -> p a q", a=DT),
                        in_=w1g[:, mt * 128:(mt + 1) * 128].rearrange(
                            "(a p) m -> p a m", p=128)).then_inc(sm["wl"].h, 16)
                    sm["wl"].n += 32
                wready[("f", seq3)] = sm["wl"].n
                sl = (seq3 % 2) * 2048

                @bk.tensor
                def _(t, s=seq3, sl=sl):
                    t.wait_ge(sm["wl"].h, wready[("f", s)])
                    t.wait_ge(sm["vec"].h, sm["vec"].n)
                    t.wait_ge(sm["sc"].h, sm["sc"].n)
                    for kt in range(DT):
                        sm["mm"].inc(t.matmul(psA[:, 0:512],
                                              wbuf[:, sl + kt * 128:sl + kt * 128 + 128],
                                              XS[kt], start=(kt == 0),
                                              stop=(kt == DT - 1)))
                    for kt in range(DT):
                        sm["mm"].inc(t.matmul(psB[:, 0:512],
                                              wbuf[:, sl + 1024 + kt * 128:
                                                   sl + 1024 + kt * 128 + 128],
                                              XS[kt], start=(kt == 0),
                                              stop=(kt == DT - 1)))

                @bk.vector
                def _(v, mt=mt):
                    v.wait_ge(sm["mm"].h, sm["mm"].n)
                    sm["vec"].inc(v.tensor_scalar_add(s1[:, :], psB[:, 0:512],
                                                      B1G[:, mt:mt + 1]))

                @bk.scalar
                def _(sc):
                    sc.wait_ge(sm["vec"].h, sm["vec"].n)
                    sm["sc"].inc(sc.activation(s2[:, :], s1[:, :], AF.Gelu))

                @bk.vector
                def _(v, mt=mt):
                    v.wait_ge(sm["sc"].h, sm["sc"].n)
                    sm["vec"].inc(v.scalar_tensor_tensor(
                        HT[mt][:, :], psA[:, 0:512], B1A[:, mt:mt + 1], s2[:, :],
                        ALU.add, ALU.mult))
                wfree[("f", seq3)] = sm["mm"].n
                seq3 += 1
            h_done = sm["vec"].n

            for mt in range(DT):
                @bk.gpsimd
                def _(g, mt=mt, s=seq3, fc=fc):
                    if s >= 2:
                        g.wait_ge(sm["mm"].h, wfree[("f", s - 2)])
                    g.dma_start(
                        out=wbuf2[:, (s % 2) * FH:(s % 2) * FH + FH].rearrange(
                            "p (a q) -> p a q", a=FHT),
                        in_=w2p[:, mt * 128:(mt + 1) * 128].rearrange(
                            "(a p) m -> p a m", p=128)).then_inc(sm["wl"].h, 16)
                    sm["wl"].n += 16
                wready[("f", seq3)] = sm["wl"].n
                sl2 = (seq3 % 2) * FH

                @bk.tensor
                def _(t, s=seq3, sl2=sl2):
                    t.wait_ge(sm["wl"].h, wready[("f", s)])
                    t.wait_ge(sm["vec"].h, sm["vec"].n)
                    for kt in range(FHT):
                        sm["mm"].inc(t.matmul(psA[:, 0:512],
                                              wbuf2[:, sl2 + kt * 128:
                                                    sl2 + kt * 128 + 128],
                                              HT[kt][:, :], start=(kt == 0),
                                              stop=(kt == FHT - 1)))

                @bk.vector
                def _(v, mt=mt):
                    v.wait_ge(sm["mm"].h, sm["mm"].n)
                    v.wait_ge(sm["st"].h, sm["st"].n)
                    sm["vec"].inc(v.scalar_tensor_tensor(
                        t1[:, :], psA[:, 0:512], B2C[:, mt:mt + 1],
                        YH[mt][:, 1:513], ALU.add, ALU.add))
                vn = sm["vec"].n

                @bk.sync
                def _(s, mt=mt, fc=fc, vn=vn):
                    s.wait_ge(sm["vec"].h, vn)
                    s.dma_start(out=xo[mt * 128:(mt + 1) * 128,
                                       fc * 512:(fc + 1) * 512],
                                in_=t1[:, :]).then_inc(sm["st"].h, 16)
                sm["st"].n += 16
                wfree[("f", seq3)] = sm["mm"].n
                seq3 += 1

        @bk.sync
        def _(s):
            s.wait_ge(sm["st"].h, sm["st"].n)

    ctx.close()
    return nc


def _l1f_consts():
    half = np.arange(0, 64, 2, dtype=np.float64)
    inv_freq = 1.0 / 10000.0 ** (half / 64.0)
    base_scale = (half + 0.4 * 64) / (1.4 * 64)
    rot = np.zeros((128, 128), np.float32)
    for base in (0, 64):
        for d in range(32):
            rot[base + d + 32, base + d] = -1.0
            rot[base + d, base + d + 32] = 1.0
    hsa = np.zeros((128, 2), np.float32)
    hsa[0:64, 0] = 1.0
    hsa[64:128, 1] = 1.0
    hsb = np.zeros((2, 128), np.float32)
    hsb[0, 0:64] = 1.0
    hsb[1, 64:128] = 1.0
    idm = np.eye(128, dtype=np.float32)
    return inv_freq, base_scale, rot, hsa, hsb, idm


def host_l1f_percore():
    inv_freq, base_scale, rot, hsa, hsb, idm = _l1f_consts()
    percore = []
    for core in range(8):
        b, q = core // 4, core % 4
        T0 = q * TC
        t = np.arange(T0 - 512, T0 + TC, dtype=np.float64)
        fr = t[:, None] * inv_freq[None, :]
        fr2 = np.concatenate([fr, fr], 1)
        power = (t - S // 2) / (256 // 2)
        scl = base_scale[None, :] ** power[:, None]
        scl2 = np.concatenate([scl, scl], 1)
        coss = (np.cos(fr2) * scl2).T
        sins = (np.sin(fr2) * scl2).T
        cosi = (np.cos(fr2) / scl2).T
        sini = (np.sin(fr2) / scl2).T
        tabs = np.zeros((128, 4 * KSP), np.float32)
        for k, tb in enumerate([coss, sins, cosi, sini]):
            tabs[0:64, k * KSP:(k + 1) * KSP] = tb
            tabs[64:128, k * KSP:(k + 1) * KSP] = tb
        mqb = np.full((128, NB1 * 2 * 512), -1e38, np.float32)
        mkt = np.zeros((128, NB1 * 4 * 256), np.float32)
        for c in range(NB1):
            Sb = T0 - 256 + 256 * c
            ii = np.arange(256)
            jj = np.arange(512)
            valid = ((ii[:, None] <= jj[None, :])
                     & (jj[None, :] <= ii[:, None] + 256)
                     & (Sb - 256 + jj[None, :] >= 0)
                     & (Sb + ii[:, None] >= 0))
            for qt in range(2):
                blk = np.where(valid[qt * 128:(qt + 1) * 128, :], 0.0, -1e38)
                mqb[:, (c * 2 + qt) * 512:(c * 2 + qt + 1) * 512] = blk
            for kt in range(4):
                blk = valid[:, kt * 128:(kt + 1) * 128].T.astype(np.float32)
                mkt[:, (c * 4 + kt) * 256:(c * 4 + kt + 1) * 256] = blk
        percore.append({"tabs": tabs, "mqb": mqb.astype(BF),
                        "mkt": mkt.astype(BF), "rotq": rot, "hsa": hsa,
                        "hsb": hsb, "idm": idm})
    return percore


def host_l1f_layer(i, w):
    wq = np.ascontiguousarray(w["Wqkv"][i].astype(np.float32))
    wo = np.ascontiguousarray(w["Wo"][i].astype(np.float32))
    W1 = w["W1"][i]
    w1a = np.zeros((D, FH), np.float32)
    w1g = np.zeros((D, FH), np.float32)
    w1a[:, 0:2730] = W1[:, 0:2730]
    w1g[:, 0:2730] = W1[:, 2730:5460]
    w2p = np.zeros((FH, D), np.float32)
    w2p[0:2730, :] = w["W2"][i]
    def cols(vec, n):
        out = np.zeros((128, n), np.float32)
        m = vec.shape[0]
        for t in range(n):
            lo = t * 128
            if lo < m:
                out[0:min(128, m - lo), t] = vec[lo:lo + 128]
        return out
    b1 = w["b1"][i]
    qk2 = np.zeros((128, 2), np.float32)
    qk2[:, 0] = np.tile(w["q_scale"][i], 2)
    qk2[:, 1] = np.tile(w["k_scale"][i], 2)
    return {"wqkvT": wq, "woT": wo, "w1a": w1a, "w1g": w1g, "w2p": w2p,
            "b1ac": cols(b1[0:2730], FHT), "b1gc": cols(b1[2730:5460], FHT),
            "b2c": cols(w["b2"][i], DT), "glnc": cols(w["ln_g"][i], DT),
            "blnc": cols(w["ln_b"][i], DT), "qk2": qk2}


def host_l1f_x(x):
    feeds = []
    for core in range(8):
        b, q = core // 4, core % 4
        t0 = q * TC
        xh = np.zeros((D, T1F), np.float32)
        lo = max(0, t0 - H1F)
        xh[:, H1F - (t0 - lo):] = x[b, lo:t0 + TC, :].T
        feeds.append(np.ascontiguousarray(xh))
    return feeds


def run_l1f(x, w, nc=None, percore=None, layers=range(6)):
    nc = nc or build_l1f()
    percore = percore or host_l1f_percore()
    for i in layers:
        lw = host_l1f_layer(i, w)
        xf = host_l1f_x(x)
        feeds = []
        for core in range(8):
            f = dict(percore[core])
            f.update(lw)
            f["xh"] = xf[core]
            feeds.append(f)
        r = run_bass_kernel_spmd(nc, feeds, core_ids=list(range(8)))
        xn = np.zeros((2, S, D), np.float32)
        for core in range(8):
            b, q = core // 4, core % 4
            xn[b, q * TC:(q + 1) * TC, :] = r.results[core]["xo"].T
        x = xn
    return x


def build_lnp():
    ctx = Ctx(psum_banks=2)
    nc = ctx.nc
    sm = ctx.sems
    sb = ctx.sb
    xh = nc.dram_tensor("xh", [D, TC], F32, kind="ExternalInput").ap()
    onesd = nc.dram_tensor("onesc", [128, DT], F32, kind="ExternalInput").ap()
    zerod = nc.dram_tensor("zeroc", [128, DT], F32, kind="ExternalInput").ap()
    xo = nc.dram_tensor("xo", [D, TC], F32, kind="ExternalOutput").ap()
    Xb, XC = ctx.tiles("xc", DT, 512, F32)
    Ob, OC = ctx.tiles("oc", DT, 512, F32)
    ONE = sb("onec", [128, DT], F32)
    ZER = sb("zerc", [128, DT], F32)
    scr = sb("scr", [128, 512], F32)
    srow = sb("srow", [1, 1536], F32)
    psA, psB = ctx.ps
    with nc.Block() as bk:
        @bk.vector
        def _(v):
            v.memset(ctx.ones_col[:, :], 1.0)
            sm["vec"].inc(v.memset(ctx.ones_row[:, :], 1.0))

        @bk.sync
        def _(s):
            s.dma_start(out=ONE[:, :], in_=onesd).then_inc(sm["ld"].h, 16)
            s.dma_start(out=ZER[:, :], in_=zerod).then_inc(sm["ld"].h, 16)
        sm["ld"].n += 32

        for c in range(4):
            @bk.sync
            def _(s, c=c):
                s.wait_ge(sm["vec"].h, sm["vec"].n)
                s.wait_ge(sm["st"].h, sm["st"].n)
                dma_tiles(s, sm["ld"], Xb, DT, xh[:, c * 512:(c + 1) * 512])

            @bk.tensor
            def _(t):
                t.wait_ge(sm["ld"].h, sm["ld"].n)

            @bk.scalar
            def _(sc):
                sc.wait_ge(sm["ld"].h, sm["ld"].n)

            emit_ln_gb(ctx, bk, XC, OC, 0, 512, psA, psB, scr, srow, ONE, ZER)
            vn = sm["vec"].n

            @bk.sync
            def _(s, c=c, vn=vn):
                s.wait_ge(sm["vec"].h, vn)
                s.dma_start(out=xo[:, c * 512:(c + 1) * 512].rearrange(
                    "(a p) t -> p a t", p=128),
                    in_=Ob.rearrange("p (a t) -> p a t", a=DT)
                    ).then_inc(sm["st"].h, 16)
            sm["st"].n += 16

        @bk.sync
        def _(s):
            s.wait_ge(sm["st"].h, sm["st"].n)
    ctx.close()
    return nc


def build_addr():
    ctx = Ctx(psum_banks=2)
    nc = ctx.nc
    sm = ctx.sems
    sb = ctx.sb
    ad = nc.dram_tensor("ad", [D, TC], F32, kind="ExternalInput").ap()
    bd = nc.dram_tensor("bd", [D, TC], F32, kind="ExternalInput").ap()
    xo = nc.dram_tensor("xo", [D, TC], F32, kind="ExternalOutput").ap()
    Ab, AC = ctx.tiles("ac", DT, 512, F32)
    Bb, BC = ctx.tiles("bc", DT, 512, F32)
    Ob, OC = ctx.tiles("oc", DT, 512, F32)
    with nc.Block() as bk:
        for c in range(4):
            @bk.sync
            def _(s, c=c):
                s.wait_ge(sm["st"].h, sm["st"].n)
                dma_tiles(s, sm["ld"], Ab, DT, ad[:, c * 512:(c + 1) * 512])
                dma_tiles(s, sm["ld"], Bb, DT, bd[:, c * 512:(c + 1) * 512])

            @bk.vector
            def _(v):
                v.wait_ge(sm["ld"].h, sm["ld"].n)
                for kt in range(DT):
                    ins = v.tensor_add(OC[kt], AC[kt], BC[kt])
                sm["vec"].inc(ins)
            vn = sm["vec"].n

            @bk.sync
            def _(s, c=c, vn=vn):
                s.wait_ge(sm["vec"].h, vn)
                s.dma_start(out=xo[:, c * 512:(c + 1) * 512].rearrange(
                    "(a p) t -> p a t", p=128),
                    in_=Ob.rearrange("p (a t) -> p a t", a=DT)
                    ).then_inc(sm["st"].h, 16)
            sm["st"].n += 16

        @bk.sync
        def _(s):
            s.wait_ge(sm["st"].h, sm["st"].n)
    ctx.close()
    return nc


def _tok_feeds(x, key):
    feeds = []
    for core in range(8):
        b, q = core // 4, core % 4
        feeds.append({key: np.ascontiguousarray(x[b, q * TC:(q + 1) * TC, :].T)})
    return feeds


def _tok_gather(results):
    out = np.zeros((2, S, D), np.float32)
    for core in range(8):
        b, q = core // 4, core % 4
        out[b, q * TC:(q + 1) * TC, :] = results[core]["xo"].T
    return out


def run_lnp(x, nc=None):
    nc = nc or build_lnp()
    feeds = _tok_feeds(x, "xh")
    onec = np.ones((128, DT), np.float32)
    zerc = np.zeros((128, DT), np.float32)
    for f in feeds:
        f["onesc"] = onec
        f["zeroc"] = zerc
    r = run_bass_kernel_spmd(nc, feeds, core_ids=list(range(8)))
    return _tok_gather(r.results)


def run_addr(a, b, nc=None):
    nc = nc or build_addr()
    fa = _tok_feeds(a, "ad")
    fb = _tok_feeds(b, "bd")
    feeds = [{**fa[i], **fb[i]} for i in range(8)]
    r = run_bass_kernel_spmd(nc, feeds, core_ids=list(range(8)))
    return _tok_gather(r.results)
